# version 01
# baseline (speedup 1.0000x reference)
import contextlib
import numpy as np
import ml_dtypes
import concourse.bass as bass
import concourse.mybir as mybir
from concourse.bass_utils import run_bass_kernel_spmd

F32 = mybir.dt.float32
BF16 = mybir.dt.bfloat16
AF = mybir.ActivationFunctionType
ALU = mybir.AluOpType
AX = mybir.AxisListType

D = 4096
B = 2
S = 4096
TPC = 1024
NBLK = 8
BLK = 512
NCC = D // 128
SC_A = 1.0 / np.sqrt(1024.0)
SC_B = 1.0 / np.sqrt(128.0)
EPS = 1e-5

BF = ml_dtypes.bfloat16
_RACE_DETECT = True


def blocks_of(r):
    return (r, 7 - r)



class Prog:

    def __init__(self, nc, stack):
        self.nc = nc
        self.stack = stack
        self.sems = {}
        self.c = {}
        self.marks = {}

    def sem(self, name):
        if name not in self.sems:
            self.sems[name] = self.stack.enter_context(self.nc.semaphore(name))
            self.c[name] = 0
        return self.sems[name]

    def inc(self, instr, name, n):
        self.c[name] += n
        instr.then_inc(self.sems[name], n)
        return self.c[name]

    def wait(self, eng, name, val=None):
        eng.wait_ge(self.sems[name], self.c[name] if val is None else val)

    def edge(self, ins, name):
        self.sem(name)
        self.inc(ins, name, 1)

    def join(self, eng, name):
        self.sem(name)
        eng.wait_ge(self.sems[name], self.c[name])

    def chan(self, name, n):
        return [self.sem(f"{name}{i}") for i in range(n)]

    def sb(self, name, shape, dtype):
        return self.stack.enter_context(self.nc.sbuf_tensor(name, shape, dtype))

    def ps(self, name, shape, dtype):
        return self.stack.enter_context(self.nc.psum_tensor(name, shape, dtype))


def pack_lhsT(w, n_in, n_out):
    assert w.shape == (n_in, n_out)
    ob, cc = n_out // 128, n_in // 128
    t = w.reshape(cc, 128, ob, 128)
    return np.ascontiguousarray(t.transpose(2, 1, 0, 3))


def pack_rhs(w, n_in, n_out):
    assert w.shape == (n_in, n_out)
    return np.ascontiguousarray(w.reshape(n_in // 128, 128, n_out).transpose(1, 0, 2))


def rope_tables(positions, hd=128, theta=10000.0):
    inv = 1.0 / theta ** (np.arange(0, hd, 2)[: hd // 2].astype(np.float32) / hd)
    f = np.outer(inv, positions.astype(np.float32)).astype(np.float32)
    c = np.cos(f)
    s = np.sin(f)
    C2 = np.concatenate([c, c], 0).astype(np.float32)
    S2 = np.concatenate([-s, s], 0).astype(np.float32)
    return C2, S2


def head_perm():
    p = np.arange(D).reshape(32, 128)
    per_head = np.concatenate([np.arange(0, 128, 2), np.arange(1, 128, 2)])
    return (p[:, per_head]).reshape(-1)



def emit_stats_from_sbuf(P, blk, xb, xsq, ps_st, ones128_bf, pre_wait=None,
                         cc_ready=None):
    def sbody(sc):
        if pre_wait is not None:
            sc.wait_ge(P.sems[pre_wait[0]], pre_wait[1])
        mm0 = P.c['mm']
        for cc in range(NCC):
            slot = cc % 2
            if cc_ready is not None:
                nm, val = cc_ready(cc)
                sc.wait_ge(P.sems[nm], val)
            if cc >= 2:
                sc.wait_ge(P.sems['mm'], mm0 + 4 * (cc - 1))
            ins = sc.activation(xsq[:, slot, 0:1024], xb[:, cc, :], AF.Square)
            P.inc(ins, 'se', 1)

    def tbody(t):
        se0 = P.c['se'] - NCC
        if 'consts_ve' in P.marks:
            t.wait_ge(P.sems['ve'], P.marks['consts_ve'])
        if pre_wait is not None:
            t.wait_ge(P.sems[pre_wait[0]], pre_wait[1])
        for cc in range(NCC):
            slot = cc % 2
            t.wait_ge(P.sems['se'], se0 + cc + 1)
            for hh in range(2):
                ins = t.matmul(ps_st[0:1, 2 * hh, :], ones128_bf[:, 0:1],
                               xb[:, cc, hh * 512:(hh + 1) * 512],
                               start=(cc == 0), stop=(cc == NCC - 1))
                P.inc(ins, 'mm', 1)
                ins = t.matmul(ps_st[0:1, 2 * hh + 1, :], ones128_bf[:, 0:1],
                               xsq[:, slot, hh * 512:(hh + 1) * 512],
                               start=(cc == 0), stop=(cc == NCC - 1))
                P.inc(ins, 'mm', 1)

    blk.scalar(sbody)
    blk.tensor(tbody)


def emit_bcast_rows(P, blk, ones1_f32, mu_row, rs_row, MUb, RSb, ps2):
    ve_base = [0]

    def tbody(t):
        P.wait(t, 'se')
        if 'consts_ve' in P.marks:
            t.wait_ge(P.sems['ve'], P.marks['consts_ve'])
        ve_base[0] = P.c['ve']
        k = 0
        for row, dst in ((mu_row, MUb), (rs_row, RSb)):
            for hh in range(2):
                if k >= 2:
                    t.wait_ge(P.sems['ve'], ve_base[0] + (k - 1))
                ins = t.matmul(ps2[:, k % 2, :],
                               ones1_f32[0:1, :], row[0:1, hh * 512:(hh + 1) * 512],
                               start=True, stop=True)
                P.inc(ins, 'mm', 1)
                k += 1

    blk.tensor(tbody)

    def vbody(v):
        base = P.c['mm'] - 4
        k = 0
        for row, dst in ((mu_row, MUb), (rs_row, RSb)):
            for hh in range(2):
                v.wait_ge(P.sems['mm'], base + k + 1)
                ins = v.tensor_copy(dst[:, hh * 512:(hh + 1) * 512],
                                    ps2[:, k % 2, :])
                P.inc(ins, 've', 1)
                k += 1

    blk.vector(vbody)


def emit_ln(P, blk, x_hbm, xbf, xsq, xf32, tmpf, xn, ones128_bf,
            ps_st, mu_row, rs_row, rows_tmp, MUb, RSb, ps_bc, ones1_f32,
            skip_stats_stream=False):
    nc = P.nc

    P.sem('xst')

    def gbody(g):
        src = x_hbm.rearrange("(cc p) t -> p cc t", p=128)
        for j in range(4):
            ins = g.dma_start(out=xbf[:, j * 8:(j + 1) * 8, :],
                              in_=src[:, j * 8:(j + 1) * 8, :])
            P.inc(ins, 'xst', 16)

    blk.gpsimd(gbody)

    emit_stats_from_sbuf(P, blk, xbf, xsq, ps_st, ones128_bf,
                         pre_wait=('xst', P.c['xst']))

    emit_ln_stats_finalize_ps(P, blk, ps_st, mu_row, rs_row, rows_tmp)
    emit_bcast_rows(P, blk, ones1_f32, mu_row, rs_row, MUb, RSb, ps_bc)

    xl = P.chan('xl', 2)
    xl0 = [P.c['xl0'], P.c['xl1']]

    def sybody(s):
        ve0 = P.c['ve']
        for cc in range(NCC):
            slot = cc % 2
            if cc >= 2:
                s.wait_ge(P.sems['ve'], ve0 + 2 * (cc - 1))
            ins = s.dma_start(out=xf32[:, slot, :],
                              in_=x_hbm[cc * 128:(cc + 1) * 128, :])
            P.inc(ins, f'xl{slot}', 16)

    def vbody(v):
        P.wait(v, 've')
        for cc in range(NCC):
            slot = cc % 2
            v.wait_ge(P.sems[f'xl{slot}'], xl0[slot] + 16 * (cc // 2 + 1))
            ins = v.tensor_sub(tmpf[:, slot, :], xf32[:, slot, :], MUb[:, :])
            P.edge(ins, 'dv')
            P.join(v, 'dv')
            ins = v.tensor_mul(xn[:, cc, :], tmpf[:, slot, :], RSb[:, :])
            P.inc(ins, 've', 2)

    blk.sync(sybody)
    blk.vector(vbody)


def emit_ln_stats_finalize_ps(P, blk, ps_st, mu_row, rs_row, rows_tmp):
    def vbody(v):
        P.wait(v, 'mm')
        for hh in range(2):
            ins = v.tensor_scalar_mul(mu_row[0:1, hh * 512:(hh + 1) * 512],
                                      ps_st[0:1, 2 * hh, :], 1.0 / D)
            P.edge(ins, 'dv')
            ins = v.tensor_scalar_mul(rows_tmp[0:1, hh * 512:(hh + 1) * 512],
                                      ps_st[0:1, 2 * hh + 1, :], 1.0 / D)
            P.edge(ins, 'dv')
        P.join(v, 'dv')
        ins = v.tensor_mul(rs_row[0:1, :], mu_row[0:1, :], mu_row[0:1, :])
        P.edge(ins, 'dv')
        P.join(v, 'dv')
        ins = v.tensor_sub(rows_tmp[0:1, :], rows_tmp[0:1, :], rs_row[0:1, :])
        P.edge(ins, 'dv')
        P.join(v, 'dv')
        ins = v.tensor_scalar_add(rows_tmp[0:1, :], rows_tmp[0:1, :], EPS)
        P.inc(ins, 've', 1)

    blk.vector(vbody)

    def sbody(sc):
        P.wait(sc, 've')
        ins = sc.activation(rows_tmp[0:1, :], rows_tmp[0:1, :], AF.Ln)
        P.edge(ins, 'sx')
        P.join(sc, 'sx')
        ins = sc.activation(rs_row[0:1, :], rows_tmp[0:1, :], AF.Exp, scale=-0.5)
        P.inc(ins, 'se', 1)

    blk.scalar(sbody)


def emit_proj_fm(P, blk, w_hbm, n_in, n_out, rhs_of_cc, wbuf,
                 ps_out, ep_scalar=None, ep_vector=None, ep_sync=None,
                 nt=1024, ob_order=None, ep_mm_extra=0, twaits=(),
                 ep_sem_of=None):
    OB = n_out // 128
    CC = n_in // 128
    TH = nt // 512
    NPS = ps_out.shape[1]
    if ob_order is None:
        ob_order = list(range(OB))
    mm0 = P.c['mm']
    P.sem('eps')
    P.sem('epv')
    if ep_sem_of is None:
        ep_sem_of = lambda ob, k: 'eps'
    cnt = {'eps': P.c['eps'], 'epv': P.c['epv']}
    prior_eps, prior_epv = P.c['eps'], P.c['epv']
    ep_records = []
    for i0, ob0 in enumerate(ob_order):
        for th0 in range(TH):
            k0 = i0 * TH + th0
            nm = ep_sem_of(ob0, k0)
            cnt[nm] += 1
            ep_records.append((nm, cnt[nm]))
    wld = P.chan('wld', 2)
    wld0 = [P.c['wld0'], P.c['wld1']]

    def gbody(g):
        for i, ob in enumerate(ob_order):
            slot = i % 2
            if i >= 2:
                g.wait_ge(P.sems['mm'], mm0 + (i - 1) * TH)
            else:
                g.wait_ge(P.sems['mm'], mm0)
            ins = g.dma_start(out=wbuf[:, slot, :, :], in_=w_hbm[ob])
            P.inc(ins, f'wld{slot}', 16)

    def tbody(t):
        for name, val in twaits:
            t.wait_ge(P.sems[name], val)
        for i, ob in enumerate(ob_order):
            slot = i % 2
            t.wait_ge(P.sems[f'wld{slot}'], wld0[slot] + 16 * (i // 2 + 1))
            for th in range(TH):
                k = i * TH + th
                pslot = k % NPS
                if k >= NPS:
                    nm, val = ep_records[k - NPS]
                    t.wait_ge(P.sems[nm], val)
                else:
                    if prior_eps:
                        t.wait_ge(P.sems['eps'], prior_eps)
                    if prior_epv:
                        t.wait_ge(P.sems['epv'], prior_epv)
                for cc in range(CC):
                    ins = t.matmul(ps_out[:, pslot, :], wbuf[:, slot, cc, :],
                                   rhs_of_cc(cc)[:, th * 512:(th + 1) * 512],
                                   start=(cc == 0), stop=(cc == CC - 1 and ep_mm_extra == 0))
                    last = ins
                P.inc(last, 'mm', 1)

    blk.gpsimd(gbody)
    blk.tensor(tbody)

    for emitter, engsel in ((ep_scalar, 'scalar'), (ep_vector, 'vector'),
                            (ep_sync, 'sync')):
        if emitter is None:
            continue

        def body(eng, emitter=emitter):
            for i, ob in enumerate(ob_order):
                for th in range(TH):
                    k = i * TH + th
                    pslot = k % NPS
                    emitter(eng, ob, th, ps_out[:, pslot, :], k,
                            mm0 + k + 1)
        getattr(blk, engsel)(body)



def emit_ln_apply(P, blk, x_hbm, xn, xf32, tmpf, MUb, RSb, ps2, ones1_f32):
    xl = P.chan('xl', 2)
    xl0 = [P.c['xl0'], P.c['xl1']]

    def sybody(sy):
        ve0 = P.c['ve']
        for cc in range(NCC):
            slot = cc % 2
            if cc >= 2:
                sy.wait_ge(P.sems['ve'], ve0 + 2 * (cc - 1))
            ins = sy.dma_start(out=xf32[:, slot, :],
                               in_=x_hbm[cc * 128:(cc + 1) * 128, :])
            P.inc(ins, f'xl{slot}', 16)

    def vbody(v):
        P.wait(v, 've')
        for cc in range(NCC):
            slot = cc % 2
            v.wait_ge(P.sems[f'xl{slot}'], xl0[slot] + 16 * (cc // 2 + 1))
            ins = v.tensor_sub(tmpf[:, slot, :], xf32[:, slot, :], MUb[:, :])
            P.edge(ins, 'dv')
            P.join(v, 'dv')
            ins = v.tensor_mul(xn[:, cc, :], tmpf[:, slot, :], RSb[:, :])
            P.inc(ins, 've', 2)

    blk.sync(sybody)
    blk.vector(vbody)


def emit_barrier(P, blk):
    vals = [(n, P.c[n]) for n in P.sems]

    def body(eng):
        for n, v in vals:
            if v > 0:
                eng.wait_ge(P.sems[n], v)
    for engsel in ('gpsimd', 'sync', 'tensor', 'scalar', 'vector'):
        getattr(blk, engsel)(body)



def make_residual_epilogue(P, x_old_hbm, x_new_hbm, xb_full, xf32, yrot,
                           bias_sb=None):
    P.chan('xo', 4)
    P.sem('eps')
    state = {'ve0': P.c['ve'], 'ep0': P.c['eps'],
             'xo0': [P.c[f'xo{i}'] for i in range(4)]}

    def ep_scalar(sc, ob, th, ps_ap, k, mmv):
        sc.wait_ge(P.sems['mm'], mmv)
        slot = k % 2
        if k >= 2:
            sc.wait_ge(P.sems['ve'], state['ve0'] + (k - 1))
        if bias_sb is not None:
            ins = sc.activation(yrot[:, slot, :], ps_ap, AF.Identity,
                                bias=bias_sb[:, ob:ob + 1])
        else:
            ins = sc.activation(yrot[:, slot, :], ps_ap, AF.Identity)
        P.inc(ins, 'eps', 1)

    def ep_vector(v, ob, th, ps_ap, k, mmv):
        slot = k % 2
        lslot = k % 4
        v.wait_ge(P.sems['eps'], state['ep0'] + k + 1)
        v.wait_ge(P.sems[f'xo{lslot}'], state['xo0'][lslot] + 16 * (k // 4 + 1))
        ins = v.tensor_add(xf32[:, lslot, :], yrot[:, slot, :], xf32[:, lslot, :])
        if xb_full is not None:
            P.edge(ins, 'dv')
            P.join(v, 'dv')
            ins = v.tensor_copy(xb_full[:, ob, th * 512:(th + 1) * 512],
                                xf32[:, lslot, :])
        P.inc(ins, 've', 1)

    P.chan('xr', 4)
    state['xr0'] = [P.c[f'xr{i}'] for i in range(4)]
    state['pre_ve'] = P.c['ve']
    state['pre_dv'] = P.c.get('dv', 0)

    def sync_body(sengine):
        sengine.wait_ge(P.sems['ve'], state['pre_ve'])
        if 'dv' in P.sems and state['pre_dv']:
            sengine.wait_ge(P.sems['dv'], state['pre_dv'])

        def load(k):
            ob, th = k // 2, k % 2
            if k >= 4:
                sengine.wait_ge(P.sems[f'xr{k % 4}'],
                                state['xr0'][k % 4] + 16 * (k // 4))
            ins = sengine.dma_start(out=xf32[:, k % 4, :],
                                    in_=x_old_hbm[ob * 128:(ob + 1) * 128,
                                                  th * 512:(th + 1) * 512])
            P.inc(ins, f'xo{k % 4}', 16)

        def store(k):
            ob, th = k // 2, k % 2
            sengine.wait_ge(P.sems['ve'], state['ve0'] + k + 1)
            ins = sengine.dma_start(out=x_new_hbm[ob * 128:(ob + 1) * 128,
                                                  th * 512:(th + 1) * 512],
                                    in_=xf32[:, k % 4, :])
            P.inc(ins, f'xr{k % 4}', 16)

        for k in range(4):
            load(k)
        for k in range(64):
            store(k)
            if k + 4 < 64:
                load(k + 4)

    return ep_scalar, ep_vector, sync_body


def emit_proj_residual(P, blk, w_hbm, n_in, x_old_hbm, x_new_hbm, xb_full,
                       xf32, yrot, wbuf, ps_out, bias_sb=None, rhs_of_cc=None,
                       twaits=()):
    eps_, epv, sync_body = make_residual_epilogue(
        P, x_old_hbm, x_new_hbm, xb_full, xf32, yrot, bias_sb)
    blk.sync(sync_body)
    emit_proj_fm(P, blk, w_hbm, n_in, D, rhs_of_cc, wbuf, ps_out,
                 ep_scalar=eps_, ep_vector=epv, twaits=twaits)



class AttnCfg:
    def __init__(self, **kw):
        self.v_load = None
        self.debug_qt = None
        self.debug_dst = None
        self.debug_dst2 = None
        self.debug_dst3 = None
        self.debug_dst4 = None
        self.__dict__.update(kw)


def emit_attention(P, blk, cfg):
    st = {
        'gi': 0,
        'bank_free': [None] * 4,
        'pb_guard': [0, 0],
        'pt_guard': 0,
        'qt': 0,
    }
    qstate = {}
    head_av_end = {}

    def scores_softmax(h, qg, qt4):
        causal = cfg.causal
        nblk_k = cfg.block_ids[qg] + 1 if causal else NBLK
        nch = nblk_k
        ext = (nblk_k - 1) * 512 + (qt4 + 1) * 128 if causal else 4096
        par = st['qt'] % 2
        nspill = max(0, nch - 4)
        widths = [512] * nch
        if causal:
            widths[nch - 1] = (qt4 + 1) * 128
        rec = {'mm': [], 'sp': [], 'slot': [], 'par': par, 'ext': ext,
               'ntile': ext // 128, 'nch': nch, 'widths': widths,
               'nspill': nspill, 'qt4': qt4}
        qstate[st['qt']] = rec
        sp_pred = [P.c['sp'] + j + 1 for j in range(nspill)]
        xp_pred = [P.c['xp'] + j + 1 for j in range(nch - nspill)]

        def tbody(t):
            if qt4 == 0 and qg == 0:
                cfg.k_wait(t, h)
            for kc in range(nch):
                slot = (st['gi'] + kc) % 4
                bf = st['bank_free'][slot]
                if bf is not None:
                    t.wait_ge(P.sems[bf[0]], bf[1])
                if kc < nspill:
                    st['bank_free'][slot] = ('sp', sp_pred[kc])
                else:
                    st['bank_free'][slot] = ('xp', xp_pred[kc - nspill])
                for dc in range(cfg.HC):
                    ins = t.matmul(cfg.ps_sc[:, slot, :],
                                   cfg.q_lhsT(h, qg, qt4, dc),
                                   cfg.k_rhs(h, kc, dc), start=(dc == 0),
                                   stop=(dc == cfg.HC - 1))
                P.inc(ins, 'mm', 1)
                rec['mm'].append(P.c['mm'])
                rec['slot'].append(slot)
        blk.tensor(tbody)

        if nspill:
            def spbody(sc):
                for kc in range(nspill):
                    sc.wait_ge(P.sems['mm'], rec['mm'][kc])
                    ins = sc.copy(cfg.S_sb[:, kc * 512:(kc + 1) * 512],
                                  cfg.ps_sc[:, rec['slot'][kc], :])
                    P.inc(ins, 'sp', 1)
                    rec['sp'].append(P.c['sp'])
                    assert P.c['sp'] == sp_pred[kc]
            blk.scalar(spbody)

        def vbody(v):
            for kc in range(nch):
                w = widths[kc]
                if kc < nspill:
                    v.wait_ge(P.sems['sp'], rec['sp'][kc])
                    src = cfg.S_sb[:, kc * 512:kc * 512 + w]
                else:
                    v.wait_ge(P.sems['mm'], rec['mm'][kc])
                    src = cfg.ps_sc[:, rec['slot'][kc], 0:w]
                    if causal and kc == nch - 1:
                        ins = v.tensor_add(src, src, cfg.mask_sb[:, qt4, 0:w])
                        P.edge(ins, 'dv')
                        P.join(v, 'dv')
                ins = v.reduce_max(cfg.mpart[:, par, kc:kc + 1], src, AX.X)
                P.edge(ins, 'dv')
            P.join(v, 'dv')
            ins = v.reduce_max(cfg.mrow[:, par, 0:1], cfg.mpart[:, par, 0:nch],
                               AX.X)
            P.edge(ins, 'dv')
            P.join(v, 'dv')
            ins = v.tensor_scalar_mul(cfg.negm[:, par, 0:1],
                                      cfg.mrow[:, par, 0:1], -float(cfg.scale))
            P.inc(ins, 've', 1)
            rec['negm_ve'] = P.c['ve']
        blk.vector(vbody)

        def xbody(sc):
            sc.wait_ge(P.sems['ve'], rec['negm_ve'])
            if st['pb_guard'][par] > 0:
                sc.wait_ge(P.sems['mm'], st['pb_guard'][par])
            for kc in range(nch):
                w = widths[kc]
                if kc < nspill:
                    src = cfg.S_sb[:, kc * 512:kc * 512 + w]
                else:
                    src = cfg.ps_sc[:, rec['slot'][kc], 0:w]
                ins = sc.activation(cfg.Pb[:, par, kc * 512:kc * 512 + w], src,
                                    AF.Exp, bias=cfg.negm[:, par, 0:1],
                                    scale=float(cfg.scale),
                                    accum_out=cfg.Dpart[:, par, kc:kc + 1])
                if kc >= nspill:
                    P.inc(ins, 'xp', 1)
                    assert P.c['xp'] == xp_pred[kc - nspill]
            rec['exp_xp'] = P.c['xp']
        blk.scalar(xbody)

        def vbody2(v):
            v.wait_ge(P.sems['xp'], rec['exp_xp'])
            ins = v.reduce_sum(cfg.Dsum[:, par, 0:1], cfg.Dpart[:, par, 0:nch],
                               AX.X)
            P.edge(ins, 'dv')
            P.join(v, 'dv')
            ins = v.reciprocal(cfg.rD[:, par, 0:1], cfg.Dsum[:, par, 0:1])
            P.inc(ins, 've', 1)
            rec['rd_ve'] = P.c['ve']
        blk.vector(vbody2)

        def pbody(g):
            g.wait_ge(P.sems['ve'], rec['rd_ve'])
            ins = g.tensor_scalar_mul(cfg.Pb[:, par, 0:ext],
                                      cfg.Pb[:, par, 0:ext],
                                      cfg.rD[:, par, 0:1])
            P.inc(ins, 'pn', 1)
            rec['pn'] = P.c['pn']
        blk.gpsimd(pbody)

        if cfg.debug_qt is not None and st['qt'] == cfg.debug_qt:
            def dbgbody(sy, par=par):
                sy.wait_ge(P.sems['pn'], rec['pn'])
                P.inc(sy.dma_start(out=cfg.debug_dst,
                                   in_=cfg.Pb[:, par, :]), 'st', 16)
            blk.sync(dbgbody)

            def dbgbody2(g, par=par):
                g.wait_ge(P.sems['pn'], rec['pn'])
                P.inc(g.dma_start(out=cfg.debug_dst2,
                                  in_=cfg.S_sb[:, 0:2048]), 'gld', 16)
                P.inc(g.dma_start(out=cfg.debug_dst3,
                                  in_=cfg.k_rhs(0, 0, 0)), 'gld', 16)
                P.inc(g.dma_start(out=cfg.debug_dst4,
                                  in_=cfg.k_rhs(0, 3, 0)), 'gld', 16)
            blk.gpsimd(dbgbody2)

        st['gi'] += nch
        st['qt'] += 1

    def transpose_evac(qt_global, gmax, first_of_group):
        rec = qstate[qt_global]
        par, ntile, qt4 = rec['par'], rec['ntile'], rec['qt4']
        ve0, se0 = P.c['ve'], P.c['se']
        nv = (ntile + 1) // 2
        ns = ntile // 2
        ev_ve = [ve0 + j + 1 for j in range(nv + (gmax - ntile))]
        ev_se = [se0 + j + 1 for j in range(ns)]
        rec['ev_ve'], rec['ev_se'] = ev_ve, ev_se
        rec['tr_mm'] = []

        def trslot(kt):
            return cfg.ps_tr[:, kt % 2, ((kt // 2) % 8) * 128:
                             (((kt // 2) % 8) + 1) * 128]

        def tbody(t):
            t.wait_ge(P.sems['pn'], rec['pn'])
            if first_of_group and st['pt_guard'] > 0:
                t.wait_ge(P.sems['mm'], st['pt_guard'])
            for kt in range(ntile):
                if kt >= 2:
                    j = kt - 2
                    if j % 2 == 0:
                        t.wait_ge(P.sems['ve'], ev_ve[j // 2])
                    else:
                        t.wait_ge(P.sems['se'], ev_se[j // 2])
                ins = t.transpose(trslot(kt),
                                  cfg.Pb[:, par, kt * 128:(kt + 1) * 128],
                                  cfg.ident[:, :])
                P.inc(ins, 'mm', 1)
                rec['tr_mm'].append(P.c['mm'])
            st['pb_guard'][par] = P.c['mm']
        blk.tensor(tbody)

        def vbody(v):
            for j, kt in enumerate(list(range(0, ntile, 2)) +
                                   list(range(ntile, gmax))):
                if kt < ntile:
                    v.wait_ge(P.sems['mm'], rec['tr_mm'][kt])
                    ins = v.tensor_copy(cfg.PT[:, kt, qt4 * 128:(qt4 + 1) * 128],
                                        trslot(kt))
                else:
                    ins = v.memset(cfg.PT[:, kt, qt4 * 128:(qt4 + 1) * 128], 0.0)
                P.inc(ins, 've', 1)
            assert P.c['ve'] == (ev_ve[-1] if ev_ve else ve0), \
                (P.c['ve'], ev_ve)
        blk.vector(vbody)

        def scbody(sc):
            for j, kt in enumerate(range(1, ntile, 2)):
                sc.wait_ge(P.sems['mm'], rec['tr_mm'][kt])
                ins = sc.copy(cfg.PT[:, kt, qt4 * 128:(qt4 + 1) * 128],
                              trslot(kt))
                P.inc(ins, 'se', 1)
            if ns:
                assert P.c['se'] == ev_se[-1], (P.c['se'], ev_se)
        blk.scalar(scbody)

    def emit_av(h, qg, gmax, qt_globals):
        av_mm = []
        v_sld_base = P.c['sld']

        def tbody(t):
            last_ve = max(qstate[q]['ev_ve'][-1] if qstate[q]['ev_ve'] else 0
                          for q in qt_globals)
            last_se = max(qstate[q]['ev_se'][-1] if qstate[q]['ev_se'] else 0
                          for q in qt_globals)
            if last_ve:
                t.wait_ge(P.sems['ve'], last_ve)
            if last_se:
                t.wait_ge(P.sems['se'], last_se)
            P.sem('epv')
            ep_b = P.c['epv']
            for dc in range(cfg.HC):
                slot = dc % 2
                if dc >= 2:
                    t.wait_ge(P.sems['epv'], ep_b + dc - 1)
                cfg.v_wait(t, h, qg, dc, v_sld_base)
                for kt in range(gmax):
                    ins = t.matmul(cfg.ps_av[:, slot, :],
                                   cfg.v_lhsT(h, kt, dc), cfg.PT[:, kt, :],
                                   start=(kt == 0), stop=(kt == gmax - 1))
                P.inc(ins, 'mm', 1)
                av_mm.append(P.c['mm'])
            st['pt_guard'] = P.c['mm']
        blk.tensor(tbody)
        if cfg.v_load is not None:
            def sybody(sy):
                cfg.v_load(sy, qg, av_mm)
            blk.sync(sybody)

        def vbody(v):
            for dc in range(cfg.HC):
                v.wait_ge(P.sems['mm'], av_mm[dc])
                cfg.out_writer(v, h, qg, dc, cfg.ps_av[:, dc % 2, :])
        blk.vector(vbody)
        return av_mm

    for h in range(cfg.heads):
        cfg.head_loads(h, head_av_end.get(h - 2, 0))
        for qg in range(cfg.n_qg):
            gmax = 4 * (cfg.block_ids[qg] + 1) if cfg.causal else 32
            q0 = st['qt']
            for qt4 in range(4):
                scores_softmax(h, qg, qt4)
                if qt4 > 0:
                    transpose_evac(q0 + qt4 - 1, gmax, qt4 == 1)
            transpose_evac(q0 + 3, gmax, False)
            av = emit_av(h, qg, gmax, [q0, q0 + 1, q0 + 2, q0 + 3])
        head_av_end[h] = av[-1]


def emit_attention_uniform(P, blk, cfg):
    st = {'pb_guard': [0, 0], 'pt_guard': 0, 'qt': 0,
          'bank_free': None, 's_reader': None}
    qstate = {}
    head_av_end = {}

    def scores_softmax(h, qg, qt4):
        nch = 4 if qg == 0 else 8
        nspill = nch - 4
        ext = nch * 512
        par = st['qt'] % 2
        rec = {'mm': [], 'sp': [], 'par': par, 'ext': ext,
               'ntile': ext // 128, 'nch': nch, 'nspill': nspill,
               'qt4': qt4}
        qstate[st['qt']] = rec
        sp_pred = [P.c['sp'] + j + 1 for j in range(nspill)]
        xp_pred = P.c['xp'] + (2 if nspill else 1)

        def tbody(t):
            if qt4 == 0 and qg == 0:
                cfg.k_wait(t, h)
            for kc in range(nch):
                slot = kc % 4
                if kc < 4 and st['bank_free'] is not None:
                    if kc == 0:
                        t.wait_ge(P.sems['xp'], st['bank_free'])
                elif kc >= 4:
                    t.wait_ge(P.sems['sp'], sp_pred[kc - 4])
                ins = t.matmul(cfg.ps_sc[:, slot, :],
                               cfg.q_lhsT(h, qg, qt4),
                               cfg.k_rhs(h, kc), start=True, stop=True)
                P.inc(ins, 'mm', 1)
                rec['mm'].append(P.c['mm'])
        blk.tensor(tbody)
        st['bank_free'] = xp_pred

        if nspill:
            prev_rd = st.get('s_reader')

            def spbody(sc):
                for kc in range(nspill):
                    sc.wait_ge(P.sems['mm'], rec['mm'][kc])
                    if kc == 0 and prev_rd:
                        sc.wait_ge(P.sems['ve'], prev_rd)
                    ins = sc.copy(cfg.S_sb[:, kc * 512:(kc + 1) * 512],
                                  cfg.ps_sc[:, kc, :])
                    P.inc(ins, 'sp', 1)
                    rec['sp'].append(P.c['sp'])
                    assert P.c['sp'] == sp_pred[kc]
            blk.scalar(spbody)

        ps_flat = cfg.ps_sc.ap().rearrange("p a b -> p (a b)")

        def vbody(v):
            v.wait_ge(P.sems['mm'], rec['mm'][nch - 1])
            ins = v.tensor_add(ps_flat, ps_flat,
                               cfg.mask_sb[:, qt4, qg, :])
            P.edge(ins, 'dv')
            P.join(v, 'dv')
            ins = v.reduce_max(cfg.mpart[:, par, 0:1], ps_flat, AX.X)
            P.edge(ins, 'dv')
            if nspill:
                v.wait_ge(P.sems['sp'], rec['sp'][-1])
                ins = v.reduce_max(cfg.mpart[:, par, 1:2],
                                   cfg.S_sb[:, 0:2048], AX.X)
                P.edge(ins, 'dv')
            P.join(v, 'dv')
            ins = v.reduce_max(cfg.mrow[:, par, 0:1],
                               cfg.mpart[:, par, 0:2 if nspill else 1], AX.X)
            P.edge(ins, 'dv')
            P.join(v, 'dv')
            ins = v.tensor_scalar_mul(cfg.negm[:, par, 0:1],
                                      cfg.mrow[:, par, 0:1], -float(cfg.scale))
            P.inc(ins, 've', 1)
            rec['negm_ve'] = P.c['ve']
            if nspill:
                st['s_reader'] = P.c['ve']
        blk.vector(vbody)

        def xbody(sc):
            sc.wait_ge(P.sems['ve'], rec['negm_ve'])
            if st['pb_guard'][par] > 0:
                sc.wait_ge(P.sems['mm'], st['pb_guard'][par])
            if nspill:
                ins = sc.activation(cfg.Pb[:, par, 0:2048],
                                    cfg.S_sb[:, 0:2048],
                                    AF.Exp, bias=cfg.negm[:, par, 0:1],
                                    scale=float(cfg.scale),
                                    accum_out=cfg.Dpart[:, par, 0:1])
                P.inc(ins, 'xp', 1)
                ins = sc.activation(cfg.Pb[:, par, 2048:4096], ps_flat,
                                    AF.Exp, bias=cfg.negm[:, par, 0:1],
                                    scale=float(cfg.scale),
                                    accum_out=cfg.Dpart[:, par, 1:2])
                P.inc(ins, 'xp', 1)
            else:
                ins = sc.activation(cfg.Pb[:, par, 0:2048], ps_flat,
                                    AF.Exp, bias=cfg.negm[:, par, 0:1],
                                    scale=float(cfg.scale),
                                    accum_out=cfg.Dpart[:, par, 0:1])
                P.inc(ins, 'xp', 1)
            assert P.c['xp'] == xp_pred
        blk.scalar(xbody)

        def vbody2(v):
            v.wait_ge(P.sems['xp'], xp_pred)
            ins = v.reduce_sum(cfg.Dsum[:, par, 0:1],
                               cfg.Dpart[:, par, 0:2 if nspill else 1], AX.X)
            P.edge(ins, 'dv')
            P.join(v, 'dv')
            ins = v.reciprocal(cfg.rD[:, par, 0:1], cfg.Dsum[:, par, 0:1])
            P.inc(ins, 've', 1)
            rec['rd_ve'] = P.c['ve']
        blk.vector(vbody2)

        def pbody(g):
            g.wait_ge(P.sems['ve'], rec['rd_ve'])
            ins = g.tensor_scalar_mul(cfg.Pb[:, par, 0:ext],
                                      cfg.Pb[:, par, 0:ext],
                                      cfg.rD[:, par, 0:1])
            P.inc(ins, 'pn', 1)
            rec['pn'] = P.c['pn']
        blk.gpsimd(pbody)

        st['qt'] += 1

    def transpose_evac(qt_global, first_of_group):
        rec = qstate[qt_global]
        par, ntile, qt4 = rec['par'], rec['ntile'], rec['qt4']
        ve0, se0 = P.c['ve'], P.c['se']
        nv = (ntile + 1) // 2
        ns = ntile // 2
        ev_ve = [ve0 + j + 1 for j in range(nv)]
        ev_se = [se0 + j + 1 for j in range(ns)]
        rec['ev_ve'], rec['ev_se'] = ev_ve, ev_se
        rec['tr_mm'] = []

        def trslot(kt):
            return cfg.ps_tr[:, kt % 2, ((kt // 2) % 8) * 128:
                             (((kt // 2) % 8) + 1) * 128]

        def tbody(t):
            t.wait_ge(P.sems['pn'], rec['pn'])
            if first_of_group and st['pt_guard'] > 0:
                t.wait_ge(P.sems['mm'], st['pt_guard'])
            for kt in range(ntile):
                if kt >= 2:
                    j = kt - 2
                    if j % 2 == 0:
                        t.wait_ge(P.sems['ve'], ev_ve[j // 2])
                    else:
                        t.wait_ge(P.sems['se'], ev_se[j // 2])
                ins = t.transpose(trslot(kt),
                                  cfg.Pb[:, par, kt * 128:(kt + 1) * 128],
                                  cfg.ident[:, :])
                P.inc(ins, 'mm', 1)
                rec['tr_mm'].append(P.c['mm'])
            st['pb_guard'][par] = P.c['mm']
        blk.tensor(tbody)

        def vbody(v):
            for kt in range(0, ntile, 2):
                v.wait_ge(P.sems['mm'], rec['tr_mm'][kt])
                ins = v.tensor_copy(cfg.PT[:, kt, qt4 * 128:(qt4 + 1) * 128],
                                    trslot(kt))
                P.inc(ins, 've', 1)
            assert P.c['ve'] == (ev_ve[-1] if ev_ve else ve0)
        blk.vector(vbody)

        def scbody(sc):
            for kt in range(1, ntile, 2):
                sc.wait_ge(P.sems['mm'], rec['tr_mm'][kt])
                ins = sc.copy(cfg.PT[:, kt, qt4 * 128:(qt4 + 1) * 128],
                              trslot(kt))
                P.inc(ins, 'se', 1)
            if ns:
                assert P.c['se'] == ev_se[-1]
        blk.scalar(scbody)

    def emit_av(h, qg, qt_globals):
        gmax = 16 if qg == 0 else 32
        av_mm = []

        def tbody(t):
            last_ve = max(qstate[q]['ev_ve'][-1] for q in qt_globals)
            last_se = max(qstate[q]['ev_se'][-1] for q in qt_globals
                          if qstate[q]['ev_se'])
            t.wait_ge(P.sems['ve'], last_ve)
            t.wait_ge(P.sems['se'], last_se)
            slot = (2 * h + qg) % 2
            if st['pt_guard'] > 0 and False:
                pass
            prev = st.get('av_writer_%d' % slot)
            if prev:
                t.wait_ge(P.sems['epv'], prev)
            for kt in range(gmax):
                ins = t.matmul(cfg.ps_av[:, slot, :],
                               cfg.v_lhsT(h, kt), cfg.PT[:, kt, :],
                               start=(kt == 0), stop=(kt == gmax - 1))
            P.inc(ins, 'mm', 1)
            av_mm.append(P.c['mm'])
            st['pt_guard'] = P.c['mm']
            st['av_writer_%d' % slot] = P.c['epv'] + 1
        blk.tensor(tbody)

        def vbody(v):
            v.wait_ge(P.sems['mm'], av_mm[0])
            cfg.out_writer(v, h, qg, cfg.ps_av[:, (2 * h + qg) % 2, :])
        blk.vector(vbody)

    for h in range(cfg.heads):
        cfg.head_loads(h, head_av_end.get(h - 2, 0))
        for qg in range(2):
            q0 = st['qt']
            for qt4 in range(4):
                scores_softmax(h, qg, qt4)
                if qt4 > 0:
                    transpose_evac(q0 + qt4 - 1, qt4 == 1)
            transpose_evac(q0 + 3, False)
            emit_av(h, qg, [q0, q0 + 1, q0 + 2, q0 + 3])
        head_av_end[h] = P.c['mm']
        if getattr(cfg, 'post_head', None) is not None:
            cfg.post_head(h)




def build_launch_A():
    nc = bass.Bass("TRN2", target_bir_lowering=False,
                   detect_race_conditions=_RACE_DETECT)
    x0 = nc.dram_tensor("x0", [D, TPC], F32, kind="ExternalInput").ap()
    wqk = nc.dram_tensor("wqk", [16, 128, 32, 128], F32, kind="ExternalInput").ap()
    wv = nc.dram_tensor("wv", [128, 32, 1024], F32, kind="ExternalInput").ap()
    bqk = nc.dram_tensor("bqk", [128, 16], F32, kind="ExternalInput").ap()
    bv = nc.dram_tensor("bv", [1, 1024], F32, kind="ExternalInput").ap()
    q1o = nc.dram_tensor("q1o", [1024, TPC], BF16, kind="ExternalOutput").ap()
    k1o = nc.dram_tensor("k1o", [1024, TPC], BF16, kind="ExternalOutput").ap()
    v1o = nc.dram_tensor("v1o", [TPC, 1024], BF16, kind="ExternalOutput").ap()

    with contextlib.ExitStack() as stack:
        P = Prog(nc, stack)
        for s in ('gld', 'sld', 'mm', 'se', 've', 'ep', 'st'):
            P.sem(s)
        xn = P.sb("xn", [128, NCC, 1024], BF16)
        xsq = P.sb("xsq", [128, 2, 1024], BF16)
        xf32 = P.sb("xf32", [128, 2, 1024], F32)
        tmpf = P.sb("tmpf", [128, 2, 1024], F32)
        wbuf = P.sb("wbuf", [128, 2, 32, 128], BF16)
        wvb = P.sb("wvb", [128, 32, 1024], BF16)
        otile = P.sb("otile", [128, 4, 512], BF16)
        ones128_bf = P.sb("ones128_bf", [128, 1], BF16)
        ones1_bf = P.sb("ones1_bf", [1, 128], BF16)
        ones1_f32 = P.sb("ones1_f32", [1, 128], F32)
        bqk_sb = P.sb("bqk_sb", [128, 16], F32)
        bv_sb = P.sb("bv_sb", [1, 1024], BF16)
        mu_row = P.sb("mu_row", [1, 1024], F32)
        rs_row = P.sb("rs_row", [1, 1024], F32)
        rows_tmp = P.sb("rows_tmp", [1, 1024], F32)
        MUb = P.sb("MUb", [128, 1024], F32)
        RSb = P.sb("RSb", [128, 1024], F32)

        ps_st = P.ps("ps_st", [1, 4, 512], F32)
        ps_out = P.ps("ps_out", [128, 2, 512], F32)
        ps_v = P.ps("ps_v", [128, 2, 512], F32)
        ps_bc = ps_out

        with nc.Block() as blk:
            def vinit(v):
                P.inc(v.memset(ones128_bf[:, :], 1.0), 've', 1)
                P.inc(v.memset(ones1_bf[0:1, :], 1.0), 've', 1)
                P.inc(v.memset(ones1_f32[0:1, :], 1.0), 've', 1)
            blk.vector(vinit)
            P.marks['consts_ve'] = P.c['ve']

            P.sem('ini')
            P.sem('cst')

            def sinit(s):
                P.inc(s.dma_start(out=bqk_sb[:, :], in_=bqk), 'ini', 16)
            blk.sync(sinit)

            def ginit(g):
                P.inc(g.dma_start(out=bv_sb[0:1, :], in_=bv), 'cst', 16)
                P.inc(g.dma_start(out=wvb[:, :, :], in_=wv), 'cst', 16)
            blk.gpsimd(ginit)

            emit_ln(P, blk, x0, xn, xsq, xf32, tmpf, xn, ones128_bf,
                    ps_st, mu_row, rs_row, rows_tmp, MUb, RSb, ps_bc, ones1_f32)

            P.chan('sto', 4)

            def ep_scalar(sc, ob, th, ps_ap, k, mmv):
                sc.wait_ge(P.sems['mm'], mmv)
                if k == 0:
                    P.wait(sc, 'ini')
                oslot = k % 4
                sc.wait_ge(P.sems[f'sto{oslot}'],
                           sto_base[0][oslot] + 16 * (k // 4))
                ins = sc.activation(otile[:, oslot, :], ps_ap, AF.Identity,
                                    bias=bqk_sb[:, ob:ob + 1])
                P.inc(ins, 'eps', 1)

            def ep_sync(s, ob, th, ps_ap, k, mmv):
                s.wait_ge(P.sems['eps'], ep_base[0] + k + 1)
                oslot = k % 4
                dst = q1o if ob < 8 else k1o
                rb = (ob % 8) * 128
                ins = s.dma_start(out=dst[rb:rb + 128, th * 512:(th + 1) * 512],
                                  in_=otile[:, oslot, :])
                P.inc(ins, f'sto{oslot}', 16)

            P.sem('eps')
            ep_base = [P.c['eps']]
            sto_base = [[P.c[f'sto{i}'] for i in range(4)]]
            emit_proj_fm(P, blk, wqk, D, 2048, lambda cc: xn[:, cc, :], wbuf,
                         ps_out, ep_scalar=ep_scalar, ep_sync=ep_sync,
                         twaits=(('ve', P.c['ve']),))

            def tbody(t):
                P.wait(t, 'cst')
                t.wait_ge(P.sems['ve'], P.marks['consts_ve'])
                mmv = P.c['mm']
                P.sem('epv')
                ep_b = P.c['epv']
                k = 0
                for fb in range(2):
                    for th in range(8):
                        pslot = k % 2
                        if k >= 2:
                            t.wait_ge(P.sems['epv'], ep_b + (k - 1))
                        for cc in range(NCC):
                            t.matmul(ps_v[:, pslot, :],
                                     xn[:, cc, th * 128:(th + 1) * 128],
                                     wvb[:, cc, fb * 512:(fb + 1) * 512],
                                     start=(cc == 0), stop=False)
                        ins = t.matmul(ps_v[:, pslot, :], ones1_bf[0:1, 0:128],
                                       bv_sb[0:1, fb * 512:(fb + 1) * 512],
                                       start=False, stop=True)
                        P.inc(ins, 'mm', 1)
                        k += 1
            blk.tensor(tbody)

            sto_b2 = [P.c[f'sto{i}'] for i in range(4)]

            def vbody(v):
                mm_b = P.c['mm'] - 16
                k = 0
                for fb in range(2):
                    for th in range(8):
                        oslot = k % 4
                        v.wait_ge(P.sems['mm'], mm_b + k + 1)
                        v.wait_ge(P.sems[f'sto{oslot}'],
                                  sto_b2[oslot] + 16 * (k // 4))
                        ins = v.tensor_copy(otile[:, oslot, :], ps_v[:, k % 2, :])
                        P.inc(ins, 'epv', 1)
                        k += 1
            blk.vector(vbody)

            def sybody(s):
                ep_b = P.c['epv'] - 16
                k = 0
                for fb in range(2):
                    for th in range(8):
                        oslot = k % 4
                        s.wait_ge(P.sems['epv'], ep_b + k + 1)
                        ins = s.dma_start(
                            out=v1o[th * 128:(th + 1) * 128, fb * 512:(fb + 1) * 512],
                            in_=otile[:, oslot, :])
                        P.inc(ins, f'sto{oslot}', 16)
                        k += 1
                for i in range(4):
                    P.wait(s, f'sto{i}')
            blk.sync(sybody)

    return nc


def emit_ffn_block(P, blk, nc, w12, w3, b12_sb, b3_sb, x_old_hbm, x_new_hbm,
                   ones128_bf, ones1_f32, mu_row, rs_row, rows_tmp, MUb, RSb,
                   with_stats):
    with contextlib.ExitStack() as ph:
        Pph = lambda name, shape, dt: ph.enter_context(
            nc.sbuf_tensor(name, shape, dt))
        x4 = Pph("x4", [128, 32, 1024], BF16)
        wbuf = Pph("wbufb", [128, 2, 32, 128], BF16)
        xf32 = Pph("xf32b", [128, 4, 512], F32)
        t1 = Pph("t1", [128, 4, 512], F32)
        yrot = t1[:, 0:2, :]
        ps_out = ph.enter_context(nc.psum_tensor("ps_out2", [128, 4, 512], F32))

        scope_a = contextlib.ExitStack()
        xn = scope_a.enter_context(nc.sbuf_tensor("xn", [128, 32, 1024], BF16))
        xfln = xf32.ap().rearrange("p a b -> p (a b)").rearrange(
            "p (a b) -> p a b", a=2)
        tmpf = scope_a.enter_context(nc.sbuf_tensor("tmpf", [128, 2, 1024], F32))

        emit_bcast_rows(P, blk, ones1_f32, mu_row, rs_row, MUb, RSb,
                        ps_out[:, 0:2, :])
        emit_ln_apply(P, blk, x_old_hbm, xn, xfln, tmpf, MUb, RSb,
                      ps_out, ones1_f32)

        ob_order = []
        for j in range(32):
            ob_order += [j, j + 32]

        def ep_scalar(sc, ob, th, ps_ap, k, mmv):
            sc.wait_ge(P.sems['mm'], mmv)
            if ob < 32:
                if k >= 4:
                    sc.wait_ge(P.sems['epv'], ffn_epv0[0]
                               + 2 * ((k - 4) // 4) + ((k - 4) % 4) + 1)
                ins = sc.activation(t1[:, k % 4, :], ps_ap, AF.Silu,
                                    bias=b12_sb[:, ob:ob + 1])
                P.inc(ins, 'eps', 1)

        def ep_vector(v, ob, th, ps_ap, k, mmv):
            if ob < 32:
                return
            v.wait_ge(P.sems['mm'], mmv)
            v.wait_ge(P.sems['eps'], ffn_ep0[0] + (k - 2) // 4 * 2
                      + (k - 2) % 4 + 1)
            ins = v.scalar_tensor_tensor(
                x4[:, ob - 32, th * 512:(th + 1) * 512], ps_ap,
                b12_sb[:, ob:ob + 1], t1[:, (k - 2) % 4, :],
                ALU.add, ALU.mult)
            P.inc(ins, 'epv', 1)

        P.sem('eps')
        P.sem('epv')
        ffn_ep0 = [P.c['eps']]
        ffn_epv0 = [P.c['epv']]
        emit_proj_fm(P, blk, w12, D, 8192,
                     lambda cc: xn[:, cc, :], wbuf, ps_out,
                     ep_scalar=ep_scalar, ep_vector=ep_vector,
                     ob_order=ob_order, twaits=(('ve', P.c['ve']),),
                     ep_sem_of=lambda ob, k: 'eps' if ob < 32 else 'epv')
        scope_a.close()

        if with_stats:
            x2b = Pph("x2b", [128, 32, 1024], BF16)
            xsq = Pph("xsqb", [128, 2, 1024], BF16)
        else:
            x2b = None
            xsq = None
        emit_proj_residual(P, blk, w3, D, x_old_hbm, x_new_hbm, x2b, xf32,
                           yrot, wbuf, ps_out[:, 0:2, :],
                           bias_sb=b3_sb,
                           rhs_of_cc=lambda cc: x4[:, cc, :],
                           twaits=(('epv', P.c['epv']),
                                   ('eps', P.c['eps'])))
        if with_stats:
            ps_st = ph.enter_context(nc.psum_tensor("ps_st2", [1, 4, 512], F32))
            emit_stats_from_sbuf(P, blk, x2b, xsq, ps_st, ones128_bf,
                                 pre_wait=('ve', P.c['ve']))
            emit_ln_stats_finalize_ps(P, blk, ps_st, mu_row, rs_row, rows_tmp)




def build_launch_B(upto=3):
    nc = bass.Bass("TRN2", target_bir_lowering=False,
                   detect_race_conditions=_RACE_DETECT)
    x0 = nc.dram_tensor("x0", [D, TPC], F32, kind="ExternalInput").ap()
    q1i = nc.dram_tensor("q1i", [8, 128, TPC], BF16, kind="ExternalInput").ap()
    k1g = nc.dram_tensor("k1g", [8, 128, S], BF16, kind="ExternalInput").ap()
    v1g = nc.dram_tensor("v1g", [8, 128, 32, 128], BF16, kind="ExternalInput").ap()
    wo_a = nc.dram_tensor("wo_a", [32, 128, 8, 128], F32, kind="ExternalInput").ap()
    bo = nc.dram_tensor("bo", [128, 32], F32, kind="ExternalInput").ap()
    w12 = nc.dram_tensor("w12", [64, 128, 32, 128], F32, kind="ExternalInput").ap()
    b12 = nc.dram_tensor("b12", [128, 64], F32, kind="ExternalInput").ap()
    w3 = nc.dram_tensor("w3", [32, 128, 32, 128], F32, kind="ExternalInput").ap()
    b3 = nc.dram_tensor("b3", [128, 32], F32, kind="ExternalInput").ap()
    wq3 = nc.dram_tensor("wq3", [32, 128, 32, 128], F32, kind="ExternalInput").ap()
    wk3 = nc.dram_tensor("wk3", [32, 128, 32, 128], F32, kind="ExternalInput").ap()
    wv3 = nc.dram_tensor("wv3", [128, 32, D], F32, kind="ExternalInput").ap()
    ropeC = nc.dram_tensor("ropeC", [128, TPC], F32, kind="ExternalInput").ap()
    ropeS = nc.dram_tensor("ropeS", [128, TPC], F32, kind="ExternalInput").ap()
    ident_i = nc.dram_tensor("ident_i", [128, 128], BF16, kind="ExternalInput").ap()
    x1s = nc.dram_tensor("x1s", [D, TPC], F32, kind="Internal").ap()
    x2o = nc.dram_tensor("x2o", [D, TPC], F32, kind="ExternalOutput").ap()
    q3o = nc.dram_tensor("q3o", [D, TPC], BF16, kind="ExternalOutput").ap()
    k3o = nc.dram_tensor("k3o", [D, TPC], BF16, kind="ExternalOutput").ap()
    v3o = nc.dram_tensor("v3o", [TPC, D], BF16, kind="ExternalOutput").ap()

    with contextlib.ExitStack() as stack:
        P = Prog(nc, stack)
        for sm in ('gld', 'sld', 'mm', 'se', 've', 'ep', 'st', 'sp', 'xp', 'pn'):
            P.sem(sm)
        otile = P.sb("otile", [128, 4, 512], BF16)
        ident = P.sb("ident", [128, 128], BF16)
        ones128_bf = P.sb("ones128_bf", [128, 1], BF16)
        ones1_f32 = P.sb("ones1_f32", [1, 128], F32)
        bo_sb = P.sb("bo_sb", [128, 32], F32)
        b12_sb = P.sb("b12_sb", [128, 64], F32)
        b3_sb = P.sb("b3_sb", [128, 32], F32)
        mu_row = P.sb("mu_row", [1, 1024], F32)
        rs_row = P.sb("rs_row", [1, 1024], F32)
        rows_tmp = P.sb("rows_tmp", [1, 1024], F32)
        MUb = P.sb("MUb", [128, 1024], F32)
        RSb = P.sb("RSb", [128, 1024], F32)
        def c_init(blkk):
            def vinit(v):
                P.inc(v.memset(ones128_bf[:, :], 1.0), 've', 1)
                P.inc(v.memset(ones1_f32[0:1, :], 1.0), 've', 1)
            blkk.vector(vinit)
            P.marks['consts_ve'] = P.c['ve']

            P.sem('ini')

            def syinit(sy):
                for ap_, src in ((ident, ident_i), (bo_sb, bo),
                                 (b12_sb, b12), (b3_sb, b3)):
                    P.inc(sy.dma_start(out=ap_[:, :], in_=src), 'ini', 16)
            blkk.sync(syinit)

        with nc.Block() as blk:
            c_init(blk)

            with contextlib.ExitStack() as ph:
                Pph = lambda name, shape, dt: ph.enter_context(
                    nc.sbuf_tensor(name, shape, dt))
                ao = Pph("ao", [128, 8, 1024], BF16)
                attn_scope = contextlib.ExitStack()
                Pat = lambda name, shape, dt: attn_scope.enter_context(
                    nc.sbuf_tensor(name, shape, dt))
                q1sb = Pat("q1sb", [128, 8, 1024], BF16)
                Ksb = Pat("Ksb", [128, 8, S], BF16)
                Vbuf = Pat("Vbuf", [128, 2, 32, 128], BF16)
                S_sb = Pat("S_sb", [128, 2048], F32)
                Pb = Pat("Pb", [128, 2, 4096], BF16)
                PT = Pat("PT", [128, 32, 512], BF16)
                mpart = Pat("mpart", [128, 2, 8], F32)
                Dpart = Pat("Dpart", [128, 2, 8], F32)
                mrow = Pat("mrow", [128, 2, 1], F32)
                negm = Pat("negm", [128, 2, 1], F32)
                Dsum = Pat("Dsum", [128, 2, 1], F32)
                rD = Pat("rD", [128, 2, 1], F32)
                ps_sc = attn_scope.enter_context(nc.psum_tensor("ps_sc", [128, 4, 512], F32))
                ps_tr = attn_scope.enter_context(nc.psum_tensor("ps_tr", [128, 2, 1024], BF16))
                ps_av = attn_scope.enter_context(nc.psum_tensor("ps_av", [128, 2, 512], F32))

                P.sem('ka')
                sldK = [0]

                def head_loads(h, guard):
                    def sybody(sy):
                        for j in range(4):
                            ins = sy.dma_start(
                                out=Ksb[:, j * 2:(j + 1) * 2, :],
                                in_=k1g[j * 2:(j + 1) * 2].rearrange(
                                    "dc p k -> p dc k"))
                            P.inc(ins, 'ka', 16)
                        ins = sy.dma_start(out=q1sb[:, :, :],
                                           in_=q1i.rearrange("dc p t -> p dc t"))
                        P.inc(ins, 'ka', 16)
                        sldK[0] = P.c['ka']
                    blk.sync(sybody)

                def k_wait(t, h):
                    t.wait_ge(P.sems['ka'], sldK[0])
                    P.wait(t, 'ini')

                P.chan('vb', 2)
                prev_av = [None]

                def v_load(sy, qg, av_mm):
                    for dc in range(8):
                        if dc >= 2:
                            sy.wait_ge(P.sems['mm'], av_mm[dc - 2])
                        elif prev_av[0] is not None:
                            sy.wait_ge(P.sems['mm'], prev_av[0][6 + dc])
                        ins = sy.dma_start(out=Vbuf[:, dc % 2, :, :],
                                           in_=v1g[dc])
                        P.inc(ins, f'vb{dc % 2}', 16)
                    prev_av[0] = av_mm

                vbase = [0, 0]

                def v_wait(t, h, qg, dc, v_sld_base):
                    if dc == 0:
                        vbase[0] = P.c['vb0']
                        vbase[1] = P.c['vb1']
                    t.wait_ge(P.sems[f'vb{dc % 2}'],
                              vbase[dc % 2] + 16 * (dc // 2 + 1))

                def out_writer(v, h, qg, dc, ps_ap):
                    ins = v.tensor_copy(ao[:, dc, qg * 512:(qg + 1) * 512],
                                        ps_ap)
                    P.inc(ins, 'epv', 1)

                cfg = AttnCfg(
                    heads=1, HC=8, causal=False, scale=SC_A,
                    block_ids=[0, 0], n_qg=2,
                    q_lhsT=lambda h, qg, qt4, dc: q1sb[:, dc, (qg * 4 + qt4) * 128:(qg * 4 + qt4 + 1) * 128],
                    k_rhs=lambda h, kc, dc: Ksb[:, dc, kc * 512:(kc + 1) * 512],
                    v_lhsT=lambda h, kt, dc: Vbuf[:, dc % 2, kt, :],
                    head_loads=head_loads, k_wait=k_wait, v_load=v_load,
                    v_wait=v_wait, out_writer=out_writer,
                    ps_sc=ps_sc, ps_tr=ps_tr, ps_av=ps_av, S_sb=S_sb, Pb=Pb,
                    PT=PT, mpart=mpart, Dpart=Dpart, mrow=mrow, negm=negm,
                    Dsum=Dsum, rD=rD, ident=ident, mask_sb=None,
                )
                emit_attention(P, blk, cfg)
                if upto < 1:
                    def _dump(sy):
                        P.wait(sy, 'epv')
                        for cc in range(8):
                            P.inc(sy.dma_start(
                                out=q3o[cc * 128:(cc + 1) * 128, :],
                                in_=ao[:, cc, :]), 'st', 16)
                        P.wait(sy, 'st')
                    blk.sync(_dump)
                    attn_scope.close()
                    return nc
                attn_scope.close()
                emit_barrier(P, blk)

                x1b = Pph("x1b", [128, 32, 1024], BF16)
                xf32 = Pph("xf32", [128, 4, 512], F32)
                yrot = Pph("yrot", [128, 2, 512], F32)
                wbuf8 = Pph("wbuf8", [128, 2, 8, 128], BF16)
                xsq = Pph("xsq", [128, 2, 1024], BF16)
                ps_out = ph.enter_context(nc.psum_tensor("ps_out", [128, 2, 512], F32))
                ps_st = ph.enter_context(nc.psum_tensor("ps_st", [1, 4, 512], F32))

                x1dst = x1s if upto >= 2 else x2o
                emit_proj_residual(P, blk, wo_a, 1024, x0, x1dst, x1b, xf32,
                                   yrot, wbuf8, ps_out, bias_sb=bo_sb,
                                   rhs_of_cc=lambda cc: ao[:, cc, :],
                                   twaits=(('epv', P.c['epv']),))
                emit_stats_from_sbuf(P, blk, x1b, xsq, ps_st, ones128_bf,
                                     pre_wait=('ve', P.c['ve']))
                emit_ln_stats_finalize_ps(P, blk, ps_st, mu_row, rs_row,
                                          rows_tmp)

            emit_barrier(P, blk)
            if upto < 2:
                def _fin(sy):
                    for i in range(4):
                        if f'xr{i}' in P.sems:
                            P.wait(sy, f'xr{i}')
                blk.sync(_fin)
                return nc
            emit_ffn_block(P, blk, nc, w12, w3, b12_sb, b3_sb, x1s, x2o,
                           ones128_bf, ones1_f32, mu_row, rs_row, rows_tmp,
                           MUb, RSb, with_stats=True)

            emit_barrier(P, blk)
            if upto < 3:
                def _fin(sy):
                    P.wait(sy, 'st')
                blk.sync(_fin)
                return nc
            with contextlib.ExitStack() as ph:
                Pph = lambda name, shape, dt: ph.enter_context(
                    nc.sbuf_tensor(name, shape, dt))
                xn = Pph("xn3", [128, 32, 1024], BF16)
                xf32 = Pph("xf32c", [128, 2, 1024], F32)
                tmpf = Pph("tmpfc", [128, 2, 1024], F32)
                wbuf = Pph("wbufc", [128, 2, 32, 128], BF16)
                wvb = Pph("wvb", [128, 2, 32, 512], BF16)
                C2 = Pph("C2", [128, 1024], F32)
                S2 = Pph("S2", [128, 1024], F32)
                rta = Pph("rta", [128, 2, 512], F32)
                rtb = Pph("rtb", [128, 2, 512], F32)
                ps_out = ph.enter_context(nc.psum_tensor("ps_out3", [128, 4, 512], F32))

                def syinit2(sy):
                    P.inc(sy.dma_start(out=C2[:, :], in_=ropeC), 'ini', 16)
                    P.inc(sy.dma_start(out=S2[:, :], in_=ropeS), 'ini', 16)
                blk.sync(syinit2)
                rope_ready = P.c['ini']

                emit_bcast_rows(P, blk, ones1_f32, mu_row, rs_row, MUb, RSb,
                                ps_out[:, 0:2, :])
                emit_ln_apply(P, blk, x2o, xn, xf32, tmpf, MUb, RSb, ps_out,
                              ones1_f32)

                P.chan('sto', 4)

                def mk_rope_ep(dst):
                    state = {'sto0': [P.c[f'sto{i}'] for i in range(4)]}

                    def ep_vector(v, ob, th, ps_ap, k, mmv):
                        v.wait_ge(P.sems['mm'], mmv)
                        if k == 0:
                            v.wait_ge(P.sems['ini'], rope_ready)
                        slot = k % 2
                        oslot = k % 4
                        v.wait_ge(P.sems[f'sto{oslot}'],
                                  state['sto0'][oslot] + 16 * (k // 4))
                        tsl = slice(th * 512, (th + 1) * 512)
                        ins = v.tensor_mul(rta[:, slot, :], ps_ap, C2[:, tsl])
                        P.edge(ins, 'dv')
                        ins = v.tensor_copy(rtb[0:64, slot, :], ps_ap[64:128, :])
                        P.edge(ins, 'dv')
                        ins = v.tensor_copy(rtb[64:128, slot, :], ps_ap[0:64, :])
                        P.edge(ins, 'dv')
                        P.join(v, 'dv')
                        ins = v.tensor_mul(rtb[:, slot, :], rtb[:, slot, :],
                                           S2[:, tsl])
                        P.edge(ins, 'dv')
                        P.join(v, 'dv')
                        ins = v.tensor_add(otile[:, oslot, :], rta[:, slot, :],
                                           rtb[:, slot, :])
                        P.inc(ins, 'epv', 1)

                    def ep_sync(sy, ob, th, ps_ap, k, mmv):
                        sy.wait_ge(P.sems['epv'], rope_ep0[0] + k + 1)
                        ins = sy.dma_start(
                            out=dst[ob * 128:(ob + 1) * 128,
                                    th * 512:(th + 1) * 512],
                            in_=otile[:, k % 4, :])
                        P.inc(ins, f'sto{k % 4}', 16)
                    return ep_vector, ep_sync

                P.sem('epv')
                for wsrc, dst in ((wq3, q3o), (wk3, k3o)):
                    epv, epsy = mk_rope_ep(dst)
                    rope_ep0 = [P.c['epv']]
                    emit_proj_fm(P, blk, wsrc, D, D, lambda cc: xn[:, cc, :],
                                 wbuf, ps_out[:, 0:2, :], ep_vector=epv,
                                 ep_sync=epsy, twaits=(('ve', P.c['ve']),),
                                 ep_sem_of=lambda ob, k: 'epv')

                def tbody(t):
                    P.wait(t, 've')
                    ep_b = P.c['epv']
                    k = 0
                    for fb in range(8):
                        t.wait_ge(P.sems[f'wld{fb % 2}'],
                                  wv30[fb % 2] + 16 * (fb // 2 + 1))
                        for th in range(8):
                            pslot = 2 + (k % 2)
                            if k >= 2:
                                t.wait_ge(P.sems['epv'], ep_b + (k - 1))
                            for cc in range(NCC):
                                ins = t.matmul(ps_out[:, pslot, :],
                                               xn[:, cc, th * 128:(th + 1) * 128],
                                               wvb[:, fb % 2, cc, :],
                                               start=(cc == 0),
                                               stop=(cc == NCC - 1))
                            P.inc(ins, 'mm', 1)
                            v3mm.append(P.c['mm'])
                            k += 1

                def gbody(g):
                    for fb in range(8):
                        if fb >= 2:
                            g.wait_ge(P.sems['mm'], v3guard[0] + (fb - 1) * 8)
                        else:
                            g.wait_ge(P.sems['mm'], v3guard[0])
                        ins = g.dma_start(out=wvb[:, fb % 2, :, :],
                                          in_=wv3[:, :, fb * 512:(fb + 1) * 512])
                        P.inc(ins, f'wld{fb % 2}', 16)

                v3mm = []
                v3guard = [P.c['mm']]
                wv30 = [P.c['wld0'], P.c['wld1']]
                blk.gpsimd(gbody)
                blk.tensor(tbody)

                def vbody(v):
                    sto0 = [P.c[f'sto{i}'] for i in range(4)]
                    k = 0
                    for fb in range(8):
                        for th in range(8):
                            v.wait_ge(P.sems['mm'], v3mm[k])
                            oslot = k % 4
                            v.wait_ge(P.sems[f'sto{oslot}'],
                                      sto0[oslot] + 16 * (k // 4))
                            ins = v.tensor_copy(otile[:, oslot, :],
                                                ps_out[:, 2 + (k % 2), :])
                            P.inc(ins, 'epv', 1)
                            k += 1
                blk.vector(vbody)

                def sybody(sy):
                    ep_b = P.c['epv'] - 64
                    k = 0
                    for fb in range(8):
                        for th in range(8):
                            sy.wait_ge(P.sems['epv'], ep_b + k + 1)
                            ins = sy.dma_start(
                                out=v3o[th * 128:(th + 1) * 128,
                                        fb * 512:(fb + 1) * 512],
                                in_=otile[:, k % 4, :])
                            P.inc(ins, f'sto{k % 4}', 16)
                            k += 1
                    for i in range(4):
                        P.wait(sy, f'sto{i}')
                    P.wait(sy, 'st')
                blk.sync(sybody)

    return nc


def prep_B(inputs, resA):
    x = np.asarray(inputs["L0_attn_a_x"], np.float32).reshape(B * S, D)
    Wo = np.asarray(inputs["L0_attn_a_Wo"], np.float32)
    bo_ = np.asarray(inputs["L0_attn_a_bo"], np.float32)
    w12_ = np.asarray(inputs["L0_ffn_w12"], np.float32)
    b12_ = np.asarray(inputs["L0_ffn_b12"], np.float32)
    w3_ = np.asarray(inputs["L0_ffn_w3"], np.float32)
    b3_ = np.asarray(inputs["L0_ffn_b3"], np.float32)
    wq = np.asarray(inputs["L0_attn_b_wq"], np.float32)
    wk = np.asarray(inputs["L0_attn_b_wk"], np.float32)
    wv = np.asarray(inputs["L0_attn_b_wv"], np.float32)
    perm = head_perm()
    wo_p = pack_lhsT(Wo, 1024, D)
    w12_p = pack_lhsT(np.ascontiguousarray(w12_.T), D, 8192)
    w3_p = pack_lhsT(np.ascontiguousarray(w3_.T), D, D)
    wq_p = pack_lhsT(np.ascontiguousarray(wq[:, perm]), D, D)
    wk_p = pack_lhsT(np.ascontiguousarray(wk[:, perm]), D, D)
    wv_p = pack_rhs(wv, D, D)
    bo_c = np.ascontiguousarray(bo_.reshape(32, 128).T)
    b12_c = np.ascontiguousarray(b12_.reshape(64, 128).T)
    b3_c = np.ascontiguousarray(b3_.reshape(32, 128).T)
    ident = np.eye(128, dtype=np.float32).astype(BF)

    K1 = {}
    V1 = {}
    for b in range(B):
        Kf = np.zeros((1024, S), BF)
        Vt = np.zeros((S, 1024), BF)
        for r in range(4):
            core = b * 4 + r
            k1 = resA.results[core]["k1o"]
            v1 = resA.results[core]["v1o"]
            for j, blkid in enumerate(blocks_of(r)):
                Kf[:, blkid * BLK:(blkid + 1) * BLK] = k1[:, j * BLK:(j + 1) * BLK]
                Vt[blkid * BLK:(blkid + 1) * BLK, :] = v1[j * BLK:(j + 1) * BLK, :]
        K1[b] = np.ascontiguousarray(Kf.reshape(8, 128, S))
        V1[b] = np.ascontiguousarray(
            Vt.reshape(32, 128, 8, 128).transpose(2, 1, 0, 3))
    in_maps = []
    for core in range(8):
        b, r = core // 4, core % 4
        xs = np.ascontiguousarray(_slice_tokens(x, b, r).T)
        lo, hi = blocks_of(r)
        pos = np.concatenate([np.arange(lo * BLK, (lo + 1) * BLK),
                              np.arange(hi * BLK, (hi + 1) * BLK)])
        C2, S2 = rope_tables(pos)
        in_maps.append({
            "x0": xs,
            "q1i": np.ascontiguousarray(
                resA.results[core]["q1o"].reshape(8, 128, TPC)),
            "k1g": K1[b], "v1g": V1[b],
            "wo_a": wo_p, "bo": bo_c, "w12": w12_p, "b12": b12_c,
            "w3": w3_p, "b3": b3_c, "wq3": wq_p, "wk3": wk_p, "wv3": wv_p,
            "ropeC": C2, "ropeS": S2, "ident_i": ident,
        })
    return in_maps



def build_launch_C():
    nc = bass.Bass("TRN2", target_bir_lowering=False,
                   detect_race_conditions=_RACE_DETECT)
    x2 = nc.dram_tensor("x2", [D, TPC], F32, kind="ExternalInput").ap()
    q3i = nc.dram_tensor("q3i", [32, 128, TPC], BF16, kind="ExternalInput").ap()
    k3g = nc.dram_tensor("k3g", [32, 128, S], BF16, kind="ExternalInput").ap()
    v3g = nc.dram_tensor("v3g", [32, 128, 32, 128], BF16, kind="ExternalInput").ap()
    wo_b = nc.dram_tensor("wo_b", [32, 128, 32, 128], F32, kind="ExternalInput").ap()
    w12 = nc.dram_tensor("w12", [64, 128, 32, 128], F32, kind="ExternalInput").ap()
    b12 = nc.dram_tensor("b12", [128, 64], F32, kind="ExternalInput").ap()
    w3 = nc.dram_tensor("w3", [32, 128, 32, 128], F32, kind="ExternalInput").ap()
    b3 = nc.dram_tensor("b3", [128, 32], F32, kind="ExternalInput").ap()
    masks = nc.dram_tensor("masks", [128, 4, 2, 2048], BF16, kind="ExternalInput").ap()
    ident_i = nc.dram_tensor("ident_i", [128, 128], BF16, kind="ExternalInput").ap()
    x3s = nc.dram_tensor("x3s", [D, TPC], F32, kind="Internal").ap()
    ao_s = nc.dram_tensor("ao_s", [D, TPC], BF16, kind="Internal").ap()
    x4o = nc.dram_tensor("x4o", [D, TPC], F32, kind="ExternalOutput").ap()

    with contextlib.ExitStack() as stack:
        P = Prog(nc, stack)
        for sm in ('gld', 'mm', 'se', 've', 'st', 'sp', 'xp', 'pn'):
            P.sem(sm)
        ident = P.sb("ident", [128, 128], BF16)
        ones128_bf = P.sb("ones128_bf", [128, 1], BF16)
        ones1_f32 = P.sb("ones1_f32", [1, 128], F32)
        b12_sb = P.sb("b12_sb", [128, 64], F32)
        b3_sb = P.sb("b3_sb", [128, 32], F32)
        mu_row = P.sb("mu_row", [1, 1024], F32)
        rs_row = P.sb("rs_row", [1, 1024], F32)
        rows_tmp = P.sb("rows_tmp", [1, 1024], F32)
        MUb = P.sb("MUb", [128, 1024], F32)
        RSb = P.sb("RSb", [128, 1024], F32)


        with nc.Block() as blk:
            P.sem('ini')

            def vinit(v):
                P.inc(v.memset(ones128_bf[:, :], 1.0), 've', 1)
                P.inc(v.memset(ones1_f32[0:1, :], 1.0), 've', 1)
            blk.vector(vinit)
            P.marks['consts_ve'] = P.c['ve']

            def syinit(sy):
                for ap_, src in ((ident, ident_i), (b12_sb, b12),
                                 (b3_sb, b3)):
                    P.inc(sy.dma_start(out=ap_[:, :], in_=src), 'ini', 16)
            blk.sync(syinit)

            with contextlib.ExitStack() as ph:
                Pph = lambda name, shape, dt: ph.enter_context(
                    nc.sbuf_tensor(name, shape, dt))
                attn_scope = contextlib.ExitStack()
                Pat = lambda name, shape, dt: attn_scope.enter_context(
                    nc.sbuf_tensor(name, shape, dt))
                mask_sb = Pat("mask_sb", [128, 4, 2, 2048], BF16)
                otile = Pat("otile", [128, 4, 512], BF16)
                qsb = Pat("qsb", [128, 2, 1024], BF16)
                Ksb = Pat("Ksb", [128, 2, S], BF16)
                Vbuf = Pat("Vbuf", [128, 2, 32, 128], BF16)
                S_sb = Pat("S_sb", [128, 2048], F32)
                Pb = Pat("Pb", [128, 2, 4096], BF16)
                PT = Pat("PT", [128, 32, 512], BF16)
                mpart = Pat("mpart", [128, 2, 8], F32)
                Dpart = Pat("Dpart", [128, 2, 8], F32)
                mrow = Pat("mrow", [128, 2, 1], F32)
                negm = Pat("negm", [128, 2, 1], F32)
                Dsum = Pat("Dsum", [128, 2, 1], F32)
                rD = Pat("rD", [128, 2, 1], F32)
                ps_sc = attn_scope.enter_context(
                    nc.psum_tensor("ps_sc", [128, 4, 512], F32))
                ps_tr = attn_scope.enter_context(
                    nc.psum_tensor("ps_tr", [128, 2, 1024], BF16))
                ps_av = attn_scope.enter_context(
                    nc.psum_tensor("ps_av", [128, 2, 512], F32))

                def mload(sy):
                    P.inc(sy.dma_start(out=mask_sb[:, :, :, :], in_=masks),
                          'ini', 16)
                blk.sync(mload)

                P.chan('kv', 2)
                kv0 = {0: P.c['kv0'], 1: P.c['kv1']}

                def head_loads(h, guard):
                    def sybody(sy):
                        if h >= 2 and guard:
                            sy.wait_ge(P.sems['mm'], guard)
                        slot = h % 2
                        P.inc(sy.dma_start(out=Ksb[:, slot, :], in_=k3g[h]),
                              f'kv{slot}', 16)
                        P.inc(sy.dma_start(out=Vbuf[:, slot, :, :], in_=v3g[h]),
                              f'kv{slot}', 16)
                        P.inc(sy.dma_start(out=qsb[:, slot, :], in_=q3i[h]),
                              f'kv{slot}', 16)
                    blk.sync(sybody)

                def k_wait(t, h):
                    slot = h % 2
                    t.wait_ge(P.sems[f'kv{slot}'],
                              kv0[slot] + 48 * (h // 2 + 1))
                    if h == 0:
                        P.wait(t, 'ini')
                        t.wait_ge(P.sems['ve'], P.marks['consts_ve'])

                P.chan('sto', 4)
                ow_sto0 = [P.c[f'sto{i}'] for i in range(4)]
                ow_epv0 = [0]

                def out_writer(v, h, qg, ps_ap):
                    k = 2 * h + qg
                    oslot = k % 4
                    v.wait_ge(P.sems[f'sto{oslot}'],
                              ow_sto0[oslot] + 16 * (k // 4))
                    ins = v.tensor_copy(otile[:, oslot, :], ps_ap)
                    P.inc(ins, 'epv', 1)

                P.sem('epv')
                cfg = AttnCfg(
                    heads=32, scale=SC_B,
                    q_lhsT=lambda h, qg, qt4: qsb[:, h % 2, (qg * 4 + qt4) * 128:(qg * 4 + qt4 + 1) * 128],
                    k_rhs=lambda h, kc: Ksb[:, h % 2, kc * 512:(kc + 1) * 512],
                    v_lhsT=lambda h, kt: Vbuf[:, h % 2, kt, :],
                    head_loads=head_loads, k_wait=k_wait,
                    out_writer=out_writer,
                    ps_sc=ps_sc, ps_tr=ps_tr, ps_av=ps_av, S_sb=S_sb, Pb=Pb,
                    PT=PT, mpart=mpart, Dpart=Dpart, mrow=mrow, negm=negm,
                    Dsum=Dsum, rD=rD, ident=ident, mask_sb=mask_sb,
                )
                def post_head(h):
                    def body(sy):
                        for qg in range(2):
                            k = 2 * h + qg
                            sy.wait_ge(P.sems['epv'], ow_epv0[0] + k + 1)
                            ins = sy.dma_start(
                                out=ao_s[h * 128:(h + 1) * 128,
                                         qg * 512:(qg + 1) * 512],
                                in_=otile[:, k % 4, :])
                            P.inc(ins, f'sto{k % 4}', 16)
                    blk.sync(body)
                cfg.post_head = post_head

                ow_epv0[0] = P.c['epv']
                emit_attention_uniform(P, blk, cfg)
                attn_scope.close()
                emit_barrier(P, blk)

                ao = Pph("ao", [128, 32, 1024], BF16)
                x3b = Pph("x3b", [128, 32, 1024], BF16)
                xf32 = Pph("xf32", [128, 4, 512], F32)
                yrot = Pph("yrot", [128, 2, 512], F32)
                wbuf = Pph("wbuf", [128, 2, 32, 128], BF16)
                xsq = Pph("xsq", [128, 2, 1024], BF16)
                ps_out = ph.enter_context(nc.psum_tensor("ps_out", [128, 2, 512], F32))
                ps_st = ph.enter_context(nc.psum_tensor("ps_st", [1, 4, 512], F32))

                def ao_load(sy):
                    for i in range(4):
                        P.wait(sy, f'sto{i}')
                    P.sem('aol')
                    for j in range(4):
                        ins = sy.dma_start(
                            out=ao[:, j * 8:(j + 1) * 8, :],
                            in_=ao_s.rearrange("(cc p) t -> p cc t", p=128)[
                                :, j * 8:(j + 1) * 8, :])
                        P.inc(ins, 'aol', 16)
                blk.sync(ao_load)

                emit_proj_residual(P, blk, wo_b, D, x2, x3s, x3b, xf32,
                                   yrot, wbuf, ps_out, bias_sb=None,
                                   rhs_of_cc=lambda cc: ao[:, cc, :],
                                   twaits=(('aol', P.c['aol']),))
                emit_stats_from_sbuf(P, blk, x3b, xsq, ps_st, ones128_bf,
                                     pre_wait=('ve', P.c['ve']))
                emit_ln_stats_finalize_ps(P, blk, ps_st, mu_row, rs_row,
                                          rows_tmp)

            emit_barrier(P, blk)
            emit_ffn_block(P, blk, nc, w12, w3, b12_sb, b3_sb, x3s, x4o,
                           ones128_bf, ones1_f32, mu_row, rs_row, rows_tmp,
                           MUb, RSb, with_stats=False)

            def _fin(sy):
                for i in range(4):
                    if f'xr{i}' in P.sems:
                        P.wait(sy, f'xr{i}')
            blk.sync(_fin)

    return nc


def prep_C(inputs, resB):
    wo = np.asarray(inputs["L0_attn_b_wo"], np.float32)
    w12_ = np.asarray(inputs["L0_ffn_w12"], np.float32)
    b12_ = np.asarray(inputs["L0_ffn_b12"], np.float32)
    w3_ = np.asarray(inputs["L0_ffn_w3"], np.float32)
    b3_ = np.asarray(inputs["L0_ffn_b3"], np.float32)
    wo_p = pack_lhsT(wo, D, D)
    w12_p = pack_lhsT(np.ascontiguousarray(w12_.T), D, 8192)
    w3_p = pack_lhsT(np.ascontiguousarray(w3_.T), D, D)
    b12_c = np.ascontiguousarray(b12_.reshape(64, 128).T)
    b3_c = np.ascontiguousarray(b3_.reshape(32, 128).T)
    ident = np.eye(128, dtype=np.float32).astype(BF)
    NEG = np.float32(-1e30)
    tri = np.zeros((4, 128, 512), np.float32)
    for qt4 in range(4):
        for row in range(128):
            tri[qt4, row, qt4 * 128 + row + 1:] = NEG
    masks_by_rank = {}
    for r in range(4):
        m = np.zeros((128, 4, 2, 2048), np.float32)
        for qt4 in range(4):
            for cs in range(4):
                if cs == r:
                    m[:, qt4, 0, cs * 512:(cs + 1) * 512] = tri[qt4]
                elif cs > r:
                    m[:, qt4, 0, cs * 512:(cs + 1) * 512] = NEG
                kb = 4 + cs
                if kb == 7 - r:
                    m[:, qt4, 1, cs * 512:(cs + 1) * 512] = tri[qt4]
                elif kb > 7 - r:
                    m[:, qt4, 1, cs * 512:(cs + 1) * 512] = NEG
        masks_by_rank[r] = m.astype(BF)
    K3 = {}
    V3 = {}
    for b in range(B):
        Kf = np.zeros((D, S), BF)
        Vt = np.zeros((S, D), BF)
        for r in range(4):
            core = b * 4 + r
            k3 = resB.results[core]["k3o"]
            v3 = resB.results[core]["v3o"]
            for j, blkid in enumerate(blocks_of(r)):
                Kf[:, blkid * BLK:(blkid + 1) * BLK] = k3[:, j * BLK:(j + 1) * BLK]
                Vt[blkid * BLK:(blkid + 1) * BLK, :] = v3[j * BLK:(j + 1) * BLK, :]
        K3[b] = np.ascontiguousarray(Kf.reshape(32, 128, S))
        V3[b] = np.ascontiguousarray(
            Vt.reshape(32, 128, 32, 128).transpose(2, 1, 0, 3))
    in_maps = []
    for core in range(8):
        b, r = core // 4, core % 4
        in_maps.append({
            "x2": resB.results[core]["x2o"],
            "q3i": np.ascontiguousarray(
                resB.results[core]["q3o"].reshape(32, 128, TPC)),
            "k3g": K3[b], "v3g": V3[b],
            "wo_b": wo_p, "w12": w12_p, "b12": b12_c,
            "w3": w3_p, "b3": b3_c, "masks": masks_by_rank[r],
            "ident_i": ident,
        })
    return in_maps



_LAUNCH_COUNT = 0


def _run(nc, in_maps):
    global _LAUNCH_COUNT
    _LAUNCH_COUNT += 1
    return run_bass_kernel_spmd(nc, in_maps, core_ids=list(range(8)))


def _slice_tokens(arr_bt, b, r):
    lo, hi = blocks_of(r)
    return np.concatenate([arr_bt[b * S + lo * BLK: b * S + lo * BLK + BLK],
                           arr_bt[b * S + hi * BLK: b * S + hi * BLK + BLK]], 0)


def prep_A(inputs):
    x = np.asarray(inputs["L0_attn_a_x"], np.float32).reshape(B * S, D)
    Wqkv = np.asarray(inputs["L0_attn_a_Wqkv"], np.float32)
    bqkv = np.asarray(inputs["L0_attn_a_bqkv"], np.float32)
    wqk_p = pack_lhsT(Wqkv[:, :2048], D, 2048)
    wv_p = pack_rhs(Wqkv[:, 2048:], D, 1024)
    bqk = np.ascontiguousarray(bqkv[:2048].reshape(16, 128).T)
    bv = np.ascontiguousarray(bqkv[2048:].reshape(1, 1024))
    in_maps = []
    for core in range(8):
        b, r = core // 4, core % 4
        xs = np.ascontiguousarray(_slice_tokens(x, b, r).T)
        in_maps.append({"x0": xs, "wqk": wqk_p, "wv": wv_p,
                        "bqk": bqk, "bv": bv})
    return in_maps


def kernel(**inputs):
    global _LAUNCH_COUNT
    _LAUNCH_COUNT = 0
    resA = _run(build_launch_A(), prep_A(inputs))
    resB = _run(build_launch_B(), prep_B(inputs, resA))
    resC = _run(build_launch_C(), prep_C(inputs, resB))
    out = np.empty((B, S, D), np.float32)
    for core in range(8):
        b, r = core // 4, core % 4
        x4 = resC.results[core]["x4o"]
        for j, blkid in enumerate(blocks_of(r)):
            out[b, blkid * BLK:(blkid + 1) * BLK, :] = \
                x4[:, j * BLK:(j + 1) * BLK].T
    return out


# version 02 (final)
# speedup vs baseline: 1.5062x; 1.5062x over previous
import contextlib
import numpy as np
import ml_dtypes
import concourse.bass as bass
import concourse.mybir as mybir
from concourse.bass_utils import run_bass_kernel_spmd

F32 = mybir.dt.float32
BF16 = mybir.dt.bfloat16
AF = mybir.ActivationFunctionType
ALU = mybir.AluOpType
AX = mybir.AxisListType

D = 4096
B = 2
S = 4096
TPC = 1024
NBLK = 8
BLK = 512
NCC = D // 128
SC_A = 1.0 / np.sqrt(1024.0)
SC_B = 1.0 / np.sqrt(128.0)
EPS = 1e-5

BF = ml_dtypes.bfloat16
_RACE_DETECT = True


def blocks_of(r):
    return (r, 7 - r)



class Prog:

    def __init__(self, nc, stack):
        self.nc = nc
        self.stack = stack
        self.sems = {}
        self.c = {}
        self.marks = {}

    def sem(self, name):
        if name not in self.sems:
            self.sems[name] = self.stack.enter_context(self.nc.semaphore(name))
            self.c[name] = 0
        return self.sems[name]

    def inc(self, instr, name, n):
        self.c[name] += n
        instr.then_inc(self.sems[name], n)
        return self.c[name]

    def wait(self, eng, name, val=None):
        eng.wait_ge(self.sems[name], self.c[name] if val is None else val)

    def edge(self, ins, name):
        self.sem(name)
        self.inc(ins, name, 1)

    def join(self, eng, name):
        self.sem(name)
        eng.wait_ge(self.sems[name], self.c[name])

    def chan(self, name, n):
        return [self.sem(f"{name}{i}") for i in range(n)]

    def sb(self, name, shape, dtype):
        return self.stack.enter_context(self.nc.sbuf_tensor(name, shape, dtype))

    def ps(self, name, shape, dtype):
        return self.stack.enter_context(self.nc.psum_tensor(name, shape, dtype))


def pack_lhsT(w, n_in, n_out):
    assert w.shape == (n_in, n_out)
    ob, cc = n_out // 128, n_in // 128
    t = w.reshape(cc, 128, ob, 128)
    return np.ascontiguousarray(t.transpose(2, 1, 0, 3))


def pack_rhs(w, n_in, n_out):
    assert w.shape == (n_in, n_out)
    return np.ascontiguousarray(w.reshape(n_in // 128, 128, n_out).transpose(1, 0, 2))


def rope_tables(positions, hd=128, theta=10000.0):
    inv = 1.0 / theta ** (np.arange(0, hd, 2)[: hd // 2].astype(np.float32) / hd)
    f = np.outer(inv, positions.astype(np.float32)).astype(np.float32)
    c = np.cos(f)
    s = np.sin(f)
    C2 = np.concatenate([c, c], 0).astype(np.float32)
    S2 = np.concatenate([-s, s], 0).astype(np.float32)
    return C2, S2


def head_perm():
    p = np.arange(D).reshape(32, 128)
    per_head = np.concatenate([np.arange(0, 128, 2), np.arange(1, 128, 2)])
    return (p[:, per_head]).reshape(-1)



def emit_stats_from_sbuf(P, blk, xb, xsq, ps_st, ones128_bf, pre_wait=None,
                         cc_ready=None):
    def sbody(sc):
        if pre_wait is not None:
            sc.wait_ge(P.sems[pre_wait[0]], pre_wait[1])
        mm0 = P.c['mm']
        for cc in range(NCC):
            slot = cc % 2
            if cc_ready is not None:
                nm, val = cc_ready(cc)
                sc.wait_ge(P.sems[nm], val)
            if cc >= 2:
                sc.wait_ge(P.sems['mm'], mm0 + 4 * (cc - 1))
            ins = sc.activation(xsq[:, slot, 0:1024], xb[:, cc, :], AF.Square)
            P.inc(ins, 'se', 1)

    def tbody(t):
        se0 = P.c['se'] - NCC
        if 'consts_ve' in P.marks:
            t.wait_ge(P.sems['ve'], P.marks['consts_ve'])
        if pre_wait is not None:
            t.wait_ge(P.sems[pre_wait[0]], pre_wait[1])
        for cc in range(NCC):
            slot = cc % 2
            t.wait_ge(P.sems['se'], se0 + cc + 1)
            for hh in range(2):
                ins = t.matmul(ps_st[0:1, 2 * hh, :], ones128_bf[:, 0:1],
                               xb[:, cc, hh * 512:(hh + 1) * 512],
                               start=(cc == 0), stop=(cc == NCC - 1))
                P.inc(ins, 'mm', 1)
                ins = t.matmul(ps_st[0:1, 2 * hh + 1, :], ones128_bf[:, 0:1],
                               xsq[:, slot, hh * 512:(hh + 1) * 512],
                               start=(cc == 0), stop=(cc == NCC - 1))
                P.inc(ins, 'mm', 1)

    blk.scalar(sbody)
    blk.tensor(tbody)


def emit_bcast_rows(P, blk, ones1_f32, mu_row, rs_row, MUb, RSb, ps2):
    ve_base = [0]

    def tbody(t):
        P.wait(t, 'se')
        if 'consts_ve' in P.marks:
            t.wait_ge(P.sems['ve'], P.marks['consts_ve'])
        ve_base[0] = P.c['ve']
        k = 0
        for row, dst in ((mu_row, MUb), (rs_row, RSb)):
            for hh in range(2):
                if k >= 2:
                    t.wait_ge(P.sems['ve'], ve_base[0] + (k - 1))
                ins = t.matmul(ps2[:, k % 2, :],
                               ones1_f32[0:1, :], row[0:1, hh * 512:(hh + 1) * 512],
                               start=True, stop=True)
                P.inc(ins, 'mm', 1)
                k += 1

    blk.tensor(tbody)

    def vbody(v):
        base = P.c['mm'] - 4
        k = 0
        for row, dst in ((mu_row, MUb), (rs_row, RSb)):
            for hh in range(2):
                v.wait_ge(P.sems['mm'], base + k + 1)
                ins = v.tensor_copy(dst[:, hh * 512:(hh + 1) * 512],
                                    ps2[:, k % 2, :])
                P.inc(ins, 've', 1)
                k += 1

    blk.vector(vbody)


def emit_ln(P, blk, x_hbm, xbf, xsq, xf32, tmpf, xn, ones128_bf,
            ps_st, mu_row, rs_row, rows_tmp, MUb, RSb, ps_bc, ones1_f32,
            skip_stats_stream=False):
    nc = P.nc

    P.sem('xst')

    def gbody(g):
        src = x_hbm.rearrange("(cc p) t -> p cc t", p=128)
        for j in range(4):
            ins = g.dma_start(out=xbf[:, j * 8:(j + 1) * 8, :],
                              in_=src[:, j * 8:(j + 1) * 8, :])
            P.inc(ins, 'xst', 16)

    blk.gpsimd(gbody)

    emit_stats_from_sbuf(P, blk, xbf, xsq, ps_st, ones128_bf,
                         pre_wait=('xst', P.c['xst']))

    emit_ln_stats_finalize_ps(P, blk, ps_st, mu_row, rs_row, rows_tmp)
    emit_bcast_rows(P, blk, ones1_f32, mu_row, rs_row, MUb, RSb, ps_bc)

    xl = P.chan('xl', 2)
    xl0 = [P.c['xl0'], P.c['xl1']]

    def sybody(s):
        ve0 = P.c['ve']
        for cc in range(NCC):
            slot = cc % 2
            if cc >= 2:
                s.wait_ge(P.sems['ve'], ve0 + 2 * (cc - 1))
            ins = s.dma_start(out=xf32[:, slot, :],
                              in_=x_hbm[cc * 128:(cc + 1) * 128, :])
            P.inc(ins, f'xl{slot}', 16)

    def vbody(v):
        P.wait(v, 've')
        for cc in range(NCC):
            slot = cc % 2
            v.wait_ge(P.sems[f'xl{slot}'], xl0[slot] + 16 * (cc // 2 + 1))
            ins = v.tensor_sub(tmpf[:, slot, :], xf32[:, slot, :], MUb[:, :])
            P.edge(ins, 'dv')
            P.join(v, 'dv')
            ins = v.tensor_mul(xn[:, cc, :], tmpf[:, slot, :], RSb[:, :])
            P.inc(ins, 've', 2)

    blk.sync(sybody)
    blk.vector(vbody)


def emit_ln_stats_finalize_ps(P, blk, ps_st, mu_row, rs_row, rows_tmp):
    def vbody(v):
        P.wait(v, 'mm')
        for hh in range(2):
            ins = v.tensor_scalar_mul(mu_row[0:1, hh * 512:(hh + 1) * 512],
                                      ps_st[0:1, 2 * hh, :], 1.0 / D)
            P.edge(ins, 'dv')
            ins = v.tensor_scalar_mul(rows_tmp[0:1, hh * 512:(hh + 1) * 512],
                                      ps_st[0:1, 2 * hh + 1, :], 1.0 / D)
            P.edge(ins, 'dv')
        P.join(v, 'dv')
        ins = v.tensor_mul(rs_row[0:1, :], mu_row[0:1, :], mu_row[0:1, :])
        P.edge(ins, 'dv')
        P.join(v, 'dv')
        ins = v.tensor_sub(rows_tmp[0:1, :], rows_tmp[0:1, :], rs_row[0:1, :])
        P.edge(ins, 'dv')
        P.join(v, 'dv')
        ins = v.tensor_scalar_add(rows_tmp[0:1, :], rows_tmp[0:1, :], EPS)
        P.inc(ins, 've', 1)

    blk.vector(vbody)

    def sbody(sc):
        P.wait(sc, 've')
        ins = sc.activation(rows_tmp[0:1, :], rows_tmp[0:1, :], AF.Ln)
        P.edge(ins, 'sx')
        P.join(sc, 'sx')
        ins = sc.activation(rs_row[0:1, :], rows_tmp[0:1, :], AF.Exp, scale=-0.5)
        P.inc(ins, 'se', 1)

    blk.scalar(sbody)


def emit_proj_fm(P, blk, w_hbm, n_in, n_out, rhs_of_cc, wbuf,
                 ps_out, ep_scalar=None, ep_vector=None, ep_sync=None,
                 nt=1024, ob_order=None, ep_mm_extra=0, twaits=(),
                 ep_sem_of=None):
    OB = n_out // 128
    CC = n_in // 128
    TH = nt // 512
    NPS = ps_out.shape[1]
    if ob_order is None:
        ob_order = list(range(OB))
    mm0 = P.c['mm']
    P.sem('eps')
    P.sem('epv')
    if ep_sem_of is None:
        ep_sem_of = lambda ob, k: 'eps'
    cnt = {'eps': P.c['eps'], 'epv': P.c['epv']}
    prior_eps, prior_epv = P.c['eps'], P.c['epv']
    ep_records = []
    for i0, ob0 in enumerate(ob_order):
        for th0 in range(TH):
            k0 = i0 * TH + th0
            nm = ep_sem_of(ob0, k0)
            cnt[nm] += 1
            ep_records.append((nm, cnt[nm]))
    wld = P.chan('wld', 2)
    wld0 = [P.c['wld0'], P.c['wld1']]

    def gbody(g):
        for i, ob in enumerate(ob_order):
            slot = i % 2
            if i >= 2:
                g.wait_ge(P.sems['mm'], mm0 + (i - 1) * TH)
            else:
                g.wait_ge(P.sems['mm'], mm0)
            ins = g.dma_start(out=wbuf[:, slot, :, :], in_=w_hbm[ob])
            P.inc(ins, f'wld{slot}', 16)

    def tbody(t):
        for name, val in twaits:
            t.wait_ge(P.sems[name], val)
        for i, ob in enumerate(ob_order):
            slot = i % 2
            t.wait_ge(P.sems[f'wld{slot}'], wld0[slot] + 16 * (i // 2 + 1))
            for th in range(TH):
                k = i * TH + th
                pslot = k % NPS
                if k >= NPS:
                    nm, val = ep_records[k - NPS]
                    t.wait_ge(P.sems[nm], val)
                else:
                    if prior_eps:
                        t.wait_ge(P.sems['eps'], prior_eps)
                    if prior_epv:
                        t.wait_ge(P.sems['epv'], prior_epv)
                for cc in range(CC):
                    ins = t.matmul(ps_out[:, pslot, :], wbuf[:, slot, cc, :],
                                   rhs_of_cc(cc)[:, th * 512:(th + 1) * 512],
                                   start=(cc == 0), stop=(cc == CC - 1 and ep_mm_extra == 0))
                    last = ins
                P.inc(last, 'mm', 1)

    blk.gpsimd(gbody)
    blk.tensor(tbody)

    for emitter, engsel in ((ep_scalar, 'scalar'), (ep_vector, 'vector'),
                            (ep_sync, 'sync')):
        if emitter is None:
            continue

        def body(eng, emitter=emitter):
            for i, ob in enumerate(ob_order):
                for th in range(TH):
                    k = i * TH + th
                    pslot = k % NPS
                    emitter(eng, ob, th, ps_out[:, pslot, :], k,
                            mm0 + k + 1)
        getattr(blk, engsel)(body)



def emit_ln_apply(P, blk, x_hbm, xn, xf32, tmpf, MUb, RSb, ps2, ones1_f32):
    xl = P.chan('xl', 2)
    xl0 = [P.c['xl0'], P.c['xl1']]

    def sybody(sy):
        ve0 = P.c['ve']
        for cc in range(NCC):
            slot = cc % 2
            if cc >= 2:
                sy.wait_ge(P.sems['ve'], ve0 + 2 * (cc - 1))
            ins = sy.dma_start(out=xf32[:, slot, :],
                               in_=x_hbm[cc * 128:(cc + 1) * 128, :])
            P.inc(ins, f'xl{slot}', 16)

    def vbody(v):
        P.wait(v, 've')
        for cc in range(NCC):
            slot = cc % 2
            v.wait_ge(P.sems[f'xl{slot}'], xl0[slot] + 16 * (cc // 2 + 1))
            ins = v.tensor_sub(tmpf[:, slot, :], xf32[:, slot, :], MUb[:, :])
            P.edge(ins, 'dv')
            P.join(v, 'dv')
            ins = v.tensor_mul(xn[:, cc, :], tmpf[:, slot, :], RSb[:, :])
            P.inc(ins, 've', 2)

    blk.sync(sybody)
    blk.vector(vbody)


def emit_barrier(P, blk):
    vals = [(n, P.c[n]) for n in P.sems]

    def body(eng):
        for n, v in vals:
            if v > 0:
                eng.wait_ge(P.sems[n], v)
    for engsel in ('gpsimd', 'sync', 'tensor', 'scalar', 'vector'):
        getattr(blk, engsel)(body)



def make_residual_epilogue(P, x_old_hbm, x_new_hbm, xb_full, xf32, yrot,
                           bias_sb=None):
    P.chan('xo', 4)
    P.sem('eps')
    state = {'ve0': P.c['ve'], 'ep0': P.c['eps'],
             'xo0': [P.c[f'xo{i}'] for i in range(4)]}

    def ep_scalar(sc, ob, th, ps_ap, k, mmv):
        sc.wait_ge(P.sems['mm'], mmv)
        slot = k % 2
        if k >= 2:
            sc.wait_ge(P.sems['ve'], state['ve0'] + (k - 1))
        if bias_sb is not None:
            ins = sc.activation(yrot[:, slot, :], ps_ap, AF.Identity,
                                bias=bias_sb[:, ob:ob + 1])
        else:
            ins = sc.activation(yrot[:, slot, :], ps_ap, AF.Identity)
        P.inc(ins, 'eps', 1)

    def ep_vector(v, ob, th, ps_ap, k, mmv):
        slot = k % 2
        lslot = k % 4
        v.wait_ge(P.sems['eps'], state['ep0'] + k + 1)
        v.wait_ge(P.sems[f'xo{lslot}'], state['xo0'][lslot] + 16 * (k // 4 + 1))
        ins = v.tensor_add(xf32[:, lslot, :], yrot[:, slot, :], xf32[:, lslot, :])
        if xb_full is not None:
            P.edge(ins, 'dv')
            P.join(v, 'dv')
            ins = v.tensor_copy(xb_full[:, ob, th * 512:(th + 1) * 512],
                                xf32[:, lslot, :])
        P.inc(ins, 've', 1)

    P.chan('xr', 4)
    state['xr0'] = [P.c[f'xr{i}'] for i in range(4)]
    state['pre_ve'] = P.c['ve']
    state['pre_dv'] = P.c.get('dv', 0)

    def sync_body(sengine):
        sengine.wait_ge(P.sems['ve'], state['pre_ve'])
        if 'dv' in P.sems and state['pre_dv']:
            sengine.wait_ge(P.sems['dv'], state['pre_dv'])

        def load(k):
            ob, th = k // 2, k % 2
            if k >= 4:
                sengine.wait_ge(P.sems[f'xr{k % 4}'],
                                state['xr0'][k % 4] + 16 * (k // 4))
            ins = sengine.dma_start(out=xf32[:, k % 4, :],
                                    in_=x_old_hbm[ob * 128:(ob + 1) * 128,
                                                  th * 512:(th + 1) * 512])
            P.inc(ins, f'xo{k % 4}', 16)

        def store(k):
            ob, th = k // 2, k % 2
            sengine.wait_ge(P.sems['ve'], state['ve0'] + k + 1)
            ins = sengine.dma_start(out=x_new_hbm[ob * 128:(ob + 1) * 128,
                                                  th * 512:(th + 1) * 512],
                                    in_=xf32[:, k % 4, :])
            P.inc(ins, f'xr{k % 4}', 16)

        for k in range(4):
            load(k)
        for k in range(64):
            store(k)
            if k + 4 < 64:
                load(k + 4)

    return ep_scalar, ep_vector, sync_body


def emit_proj_residual(P, blk, w_hbm, n_in, x_old_hbm, x_new_hbm, xb_full,
                       xf32, yrot, wbuf, ps_out, bias_sb=None, rhs_of_cc=None,
                       twaits=()):
    eps_, epv, sync_body = make_residual_epilogue(
        P, x_old_hbm, x_new_hbm, xb_full, xf32, yrot, bias_sb)
    blk.sync(sync_body)
    emit_proj_fm(P, blk, w_hbm, n_in, D, rhs_of_cc, wbuf, ps_out,
                 ep_scalar=eps_, ep_vector=epv, twaits=twaits)



class AttnCfg:
    def __init__(self, **kw):
        self.v_load = None
        self.debug_qt = None
        self.debug_dst = None
        self.debug_dst2 = None
        self.debug_dst3 = None
        self.debug_dst4 = None
        self.__dict__.update(kw)


def emit_attention(P, blk, cfg):
    st = {
        'gi': 0,
        'bank_free': [None] * 4,
        'pb_guard': [0, 0],
        'pt_guard': 0,
        'qt': 0,
        's_reader': None,
    }
    qstate = {}
    head_av_end = {}

    def scores_softmax(h, qg, qt4):
        causal = cfg.causal
        nblk_k = cfg.block_ids[qg] + 1 if causal else NBLK
        nch = nblk_k
        ext = (nblk_k - 1) * 512 + (qt4 + 1) * 128 if causal else 4096
        par = st['qt'] % 2
        nspill = max(0, nch - 4)
        widths = [512] * nch
        if causal:
            widths[nch - 1] = (qt4 + 1) * 128
        rec = {'mm': [], 'sp': [], 'slot': [], 'par': par, 'ext': ext,
               'ntile': ext // 128, 'nch': nch, 'widths': widths,
               'nspill': nspill, 'qt4': qt4}
        qstate[st['qt']] = rec
        sp_pred = [P.c['sp'] + j + 1 for j in range(nspill)]
        xp_pred = [P.c['xp'] + j + 1 for j in range(nch - nspill)]

        def tbody(t):
            if qt4 == 0 and qg == 0:
                cfg.k_wait(t, h)
            for kc in range(nch):
                slot = (st['gi'] + kc) % 4
                bf = st['bank_free'][slot]
                if bf is not None:
                    t.wait_ge(P.sems[bf[0]], bf[1])
                if kc < nspill:
                    st['bank_free'][slot] = ('sp', sp_pred[kc])
                else:
                    st['bank_free'][slot] = ('xp', xp_pred[kc - nspill])
                for dc in range(cfg.HC):
                    ins = t.matmul(cfg.ps_sc[:, slot, :],
                                   cfg.q_lhsT(h, qg, qt4, dc),
                                   cfg.k_rhs(h, kc, dc), start=(dc == 0),
                                   stop=(dc == cfg.HC - 1))
                P.inc(ins, 'mm', 1)
                rec['mm'].append(P.c['mm'])
                rec['slot'].append(slot)
        blk.tensor(tbody)

        if nspill:
            prev_rd = st.get('s_reader')

            def spbody(v):
                for kc in range(nspill):
                    v.wait_ge(P.sems['mm'], rec['mm'][kc])
                    if kc == 0 and prev_rd:
                        v.wait_ge(P.sems['xp'], prev_rd)
                    ins = v.tensor_copy(cfg.S_sb[:, kc * 512:(kc + 1) * 512],
                                        cfg.ps_sc[:, rec['slot'][kc], :])
                    P.inc(ins, 'sp', 1)
                    rec['sp'].append(P.c['sp'])
                    assert P.c['sp'] == sp_pred[kc]
            blk.vector(spbody)

        def vbody(v):
            for kc in range(nch):
                w = widths[kc]
                if kc < nspill:
                    v.wait_ge(P.sems['sp'], rec['sp'][kc])
                    src = cfg.S_sb[:, kc * 512:kc * 512 + w]
                else:
                    v.wait_ge(P.sems['mm'], rec['mm'][kc])
                    src = cfg.ps_sc[:, rec['slot'][kc], 0:w]
                    if causal and kc == nch - 1:
                        ins = v.tensor_add(src, src, cfg.mask_sb[:, qt4, 0:w])
                        P.edge(ins, 'dv')
                        P.join(v, 'dv')
                ins = v.reduce_max(cfg.mpart[:, par, kc:kc + 1], src, AX.X)
                P.edge(ins, 'dv')
            P.join(v, 'dv')
            ins = v.reduce_max(cfg.mrow[:, par, 0:1], cfg.mpart[:, par, 0:nch],
                               AX.X)
            P.edge(ins, 'dv')
            P.join(v, 'dv')
            ins = v.tensor_scalar_mul(cfg.negm[:, par, 0:1],
                                      cfg.mrow[:, par, 0:1], -float(cfg.scale))
            P.inc(ins, 've', 1)
            rec['negm_ve'] = P.c['ve']
        blk.vector(vbody)

        def xbody(sc):
            sc.wait_ge(P.sems['ve'], rec['negm_ve'])
            if st['pb_guard'][par] > 0:
                sc.wait_ge(P.sems['mm'], st['pb_guard'][par])
            for kc in range(nch):
                w = widths[kc]
                if kc < nspill:
                    src = cfg.S_sb[:, kc * 512:kc * 512 + w]
                else:
                    src = cfg.ps_sc[:, rec['slot'][kc], 0:w]
                ins = sc.activation(cfg.Pb[:, par, kc * 512:kc * 512 + w], src,
                                    AF.Exp, bias=cfg.negm[:, par, 0:1],
                                    scale=float(cfg.scale),
                                    accum_out=cfg.Dpart[:, par, kc:kc + 1])
                if kc >= nspill:
                    P.inc(ins, 'xp', 1)
                    assert P.c['xp'] == xp_pred[kc - nspill]
            rec['exp_xp'] = P.c['xp']
            if nspill:
                st['s_reader'] = P.c['xp']
        blk.scalar(xbody)

        def vbody2(v):
            v.wait_ge(P.sems['xp'], rec['exp_xp'])
            ins = v.reduce_sum(cfg.Dsum[:, par, 0:1], cfg.Dpart[:, par, 0:nch],
                               AX.X)
            P.edge(ins, 'dv')
            P.join(v, 'dv')
            ins = v.reciprocal(cfg.rD[:, par, 0:1], cfg.Dsum[:, par, 0:1])
            P.inc(ins, 've', 1)
            rec['rd_ve'] = P.c['ve']
        blk.vector(vbody2)

        def pbody(g):
            g.wait_ge(P.sems['ve'], rec['rd_ve'])
            ins = g.tensor_scalar_mul(cfg.Pb[:, par, 0:ext],
                                      cfg.Pb[:, par, 0:ext],
                                      cfg.rD[:, par, 0:1])
            P.inc(ins, 'pn', 1)
            rec['pn'] = P.c['pn']
        blk.gpsimd(pbody)

        if cfg.debug_qt is not None and st['qt'] == cfg.debug_qt:
            def dbgbody(sy, par=par):
                sy.wait_ge(P.sems['pn'], rec['pn'])
                P.inc(sy.dma_start(out=cfg.debug_dst,
                                   in_=cfg.Pb[:, par, :]), 'st', 16)
            blk.sync(dbgbody)

            def dbgbody2(g, par=par):
                g.wait_ge(P.sems['pn'], rec['pn'])
                P.inc(g.dma_start(out=cfg.debug_dst2,
                                  in_=cfg.S_sb[:, 0:2048]), 'gld', 16)
                P.inc(g.dma_start(out=cfg.debug_dst3,
                                  in_=cfg.k_rhs(0, 0, 0)), 'gld', 16)
                P.inc(g.dma_start(out=cfg.debug_dst4,
                                  in_=cfg.k_rhs(0, 3, 0)), 'gld', 16)
            blk.gpsimd(dbgbody2)

        st['gi'] += nch
        st['qt'] += 1

    def transpose_evac(qt_global, gmax, first_of_group):
        rec = qstate[qt_global]
        par, ntile, qt4 = rec['par'], rec['ntile'], rec['qt4']
        ve0 = P.c['ve']
        ev_ve = [ve0 + j + 1 for j in range(gmax)]
        rec['ev_ve'] = ev_ve
        rec['ev_se'] = []
        rec['tr_mm'] = []

        def trslot(kt):
            return cfg.ps_tr[:, kt % 2, ((kt // 2) % 8) * 128:
                             (((kt // 2) % 8) + 1) * 128]

        def tbody(t):
            t.wait_ge(P.sems['pn'], rec['pn'])
            if first_of_group and st['pt_guard'] > 0:
                t.wait_ge(P.sems['mm'], st['pt_guard'])
            for kt in range(ntile):
                if kt >= 2:
                    t.wait_ge(P.sems['ve'], ev_ve[kt - 2])
                ins = t.transpose(trslot(kt),
                                  cfg.Pb[:, par, kt * 128:(kt + 1) * 128],
                                  cfg.ident[:, :])
                P.inc(ins, 'mm', 1)
                rec['tr_mm'].append(P.c['mm'])
            st['pb_guard'][par] = P.c['mm']
        blk.tensor(tbody)

        def vbody(v):
            for j, kt in enumerate(list(range(ntile)) +
                                   list(range(ntile, gmax))):
                if kt < ntile:
                    v.wait_ge(P.sems['mm'], rec['tr_mm'][kt])
                    ins = v.tensor_copy(cfg.PT[:, kt, qt4 * 128:(qt4 + 1) * 128],
                                        trslot(kt))
                else:
                    ins = v.memset(cfg.PT[:, kt, qt4 * 128:(qt4 + 1) * 128], 0.0)
                P.inc(ins, 've', 1)
            assert P.c['ve'] == (ev_ve[-1] if ev_ve else ve0), \
                (P.c['ve'], ev_ve)
        blk.vector(vbody)

    def emit_av(h, qg, gmax, qt_globals):
        av_mm = []
        v_sld_base = P.c['sld']

        def tbody(t):
            last_ve = max(qstate[q]['ev_ve'][-1] if qstate[q]['ev_ve'] else 0
                          for q in qt_globals)
            last_se = max(qstate[q]['ev_se'][-1] if qstate[q]['ev_se'] else 0
                          for q in qt_globals)
            if last_ve:
                t.wait_ge(P.sems['ve'], last_ve)
            if last_se:
                t.wait_ge(P.sems['se'], last_se)
            P.sem('epv')
            ep_b = P.c['epv']
            for dc in range(cfg.HC):
                slot = dc % 2
                if dc >= 2:
                    t.wait_ge(P.sems['epv'], ep_b + dc - 1)
                cfg.v_wait(t, h, qg, dc, v_sld_base)
                for kt in range(gmax):
                    ins = t.matmul(cfg.ps_av[:, slot, :],
                                   cfg.v_lhsT(h, kt, dc), cfg.PT[:, kt, :],
                                   start=(kt == 0), stop=(kt == gmax - 1))
                P.inc(ins, 'mm', 1)
                av_mm.append(P.c['mm'])
            st['pt_guard'] = P.c['mm']
        blk.tensor(tbody)
        if cfg.v_load is not None:
            def sybody(sy):
                cfg.v_load(sy, qg, av_mm)
            blk.sync(sybody)

        def vbody(v):
            for dc in range(cfg.HC):
                v.wait_ge(P.sems['mm'], av_mm[dc])
                cfg.out_writer(v, h, qg, dc, cfg.ps_av[:, dc % 2, :])
        blk.vector(vbody)
        return av_mm

    for h in range(cfg.heads):
        cfg.head_loads(h, head_av_end.get(h - 2, 0))
        for qg in range(cfg.n_qg):
            gmax = 4 * (cfg.block_ids[qg] + 1) if cfg.causal else 32
            q0 = st['qt']
            for qt4 in range(4):
                scores_softmax(h, qg, qt4)
                if qt4 > 0:
                    transpose_evac(q0 + qt4 - 1, gmax, qt4 == 1)
            transpose_evac(q0 + 3, gmax, False)
            av = emit_av(h, qg, gmax, [q0, q0 + 1, q0 + 2, q0 + 3])
        head_av_end[h] = av[-1]


def emit_attention_uniform(P, blk, cfg):
    st = {'pb_guard': [0, 0], 'pt_guard': 0, 'qt': 0,
          'bank_free': None, 's_reader': None}
    qstate = {}
    head_av_end = {}

    def scores_softmax(h, qg, qt4):
        nch = 4 if qg == 0 else 8
        nspill = nch - 4
        ext = nch * 512
        par = st['qt'] % 2
        rec = {'mm': [], 'sp': [], 'par': par, 'ext': ext,
               'ntile': ext // 128, 'nch': nch, 'nspill': nspill,
               'qt4': qt4}
        qstate[st['qt']] = rec
        sp_pred = [P.c['sp'] + j + 1 for j in range(nspill)]
        xp_pred = P.c['xp'] + (2 if nspill else 1)

        def tbody(t):
            if qt4 == 0 and qg == 0:
                cfg.k_wait(t, h)
            for kc in range(nch):
                slot = kc % 4
                if kc < 4 and st['bank_free'] is not None:
                    if kc == 0:
                        t.wait_ge(P.sems['xp'], st['bank_free'])
                elif kc >= 4:
                    t.wait_ge(P.sems['sp'], sp_pred[kc - 4])
                ins = t.matmul(cfg.ps_sc[:, slot, :],
                               cfg.q_lhsT(h, qg, qt4),
                               cfg.k_rhs(h, kc), start=True, stop=True)
                P.inc(ins, 'mm', 1)
                rec['mm'].append(P.c['mm'])
        blk.tensor(tbody)
        st['bank_free'] = xp_pred

        if nspill:
            prev_rd = st.get('s_reader')

            def spbody(v):
                for kc in range(nspill):
                    v.wait_ge(P.sems['mm'], rec['mm'][kc])
                    if kc == 0 and prev_rd:
                        v.wait_ge(P.sems['xp'], prev_rd)
                    ins = v.tensor_copy(cfg.S_sb[:, kc * 512:(kc + 1) * 512],
                                        cfg.ps_sc[:, kc, :])
                    P.inc(ins, 'sp', 1)
                    rec['sp'].append(P.c['sp'])
                    assert P.c['sp'] == sp_pred[kc]
            blk.vector(spbody)

        ps_flat = cfg.ps_sc.ap().rearrange("p a b -> p (a b)")

        def vbody(v):
            v.wait_ge(P.sems['mm'], rec['mm'][nch - 1])
            ins = v.tensor_add(ps_flat, ps_flat,
                               cfg.mask_sb[:, qt4, qg, :])
            P.edge(ins, 'dv')
            P.join(v, 'dv')
            ins = v.reduce_max(cfg.mpart[:, par, 0:1], ps_flat, AX.X)
            P.edge(ins, 'dv')
            if nspill:
                v.wait_ge(P.sems['sp'], rec['sp'][-1])
                ins = v.reduce_max(cfg.mpart[:, par, 1:2],
                                   cfg.S_sb[:, 0:2048], AX.X)
                P.edge(ins, 'dv')
            P.join(v, 'dv')
            ins = v.reduce_max(cfg.mrow[:, par, 0:1],
                               cfg.mpart[:, par, 0:2 if nspill else 1], AX.X)
            P.edge(ins, 'dv')
            P.join(v, 'dv')
            ins = v.tensor_scalar_mul(cfg.negm[:, par, 0:1],
                                      cfg.mrow[:, par, 0:1], -float(cfg.scale))
            P.inc(ins, 've', 1)
            rec['negm_ve'] = P.c['ve']
            if nspill:
                st['s_reader'] = xp_pred
        blk.vector(vbody)

        def xbody(sc):
            sc.wait_ge(P.sems['ve'], rec['negm_ve'])
            if st['pb_guard'][par] > 0:
                sc.wait_ge(P.sems['mm'], st['pb_guard'][par])
            if nspill:
                ins = sc.activation(cfg.Pb[:, par, 0:2048],
                                    cfg.S_sb[:, 0:2048],
                                    AF.Exp, bias=cfg.negm[:, par, 0:1],
                                    scale=float(cfg.scale),
                                    accum_out=cfg.Dpart[:, par, 0:1])
                P.inc(ins, 'xp', 1)
                ins = sc.activation(cfg.Pb[:, par, 2048:4096], ps_flat,
                                    AF.Exp, bias=cfg.negm[:, par, 0:1],
                                    scale=float(cfg.scale),
                                    accum_out=cfg.Dpart[:, par, 1:2])
                P.inc(ins, 'xp', 1)
            else:
                ins = sc.activation(cfg.Pb[:, par, 0:2048], ps_flat,
                                    AF.Exp, bias=cfg.negm[:, par, 0:1],
                                    scale=float(cfg.scale),
                                    accum_out=cfg.Dpart[:, par, 0:1])
                P.inc(ins, 'xp', 1)
            assert P.c['xp'] == xp_pred
        blk.scalar(xbody)

        def vbody2(v):
            v.wait_ge(P.sems['xp'], xp_pred)
            ins = v.reduce_sum(cfg.Dsum[:, par, 0:1],
                               cfg.Dpart[:, par, 0:2 if nspill else 1], AX.X)
            P.edge(ins, 'dv')
            P.join(v, 'dv')
            ins = v.reciprocal(cfg.rD[:, par, 0:1], cfg.Dsum[:, par, 0:1])
            P.inc(ins, 've', 1)
            rec['rd_ve'] = P.c['ve']
        blk.vector(vbody2)

        def pbody(sc):
            sc.wait_ge(P.sems['ve'], rec['rd_ve'])
            ins = sc.activation(cfg.Pb[:, par, 0:ext], cfg.Pb[:, par, 0:ext],
                                AF.Identity, scale=cfg.rD[:, par, 0:1])
            P.inc(ins, 'pn', 1)
            rec['pn'] = P.c['pn']
        blk.scalar(pbody)

        st['qt'] += 1

    def transpose_evac(qt_global, first_of_group):
        rec = qstate[qt_global]
        par, ntile, qt4 = rec['par'], rec['ntile'], rec['qt4']
        ve0 = P.c['ve']
        ev_ve = [ve0 + j + 1 for j in range(ntile)]
        rec['ev_ve'] = ev_ve
        rec['ev_se'] = []
        rec['tr_mm'] = []

        def trslot(kt):
            return cfg.ps_tr[:, kt % 2, ((kt // 2) % 8) * 128:
                             (((kt // 2) % 8) + 1) * 128]

        def tbody(t):
            t.wait_ge(P.sems['pn'], rec['pn'])
            if first_of_group and st['pt_guard'] > 0:
                t.wait_ge(P.sems['mm'], st['pt_guard'])
            for kt in range(ntile):
                if kt >= 2:
                    t.wait_ge(P.sems['ve'], ev_ve[kt - 2])
                ins = t.transpose(trslot(kt),
                                  cfg.Pb[:, par, kt * 128:(kt + 1) * 128],
                                  cfg.ident[:, :])
                P.inc(ins, 'mm', 1)
                rec['tr_mm'].append(P.c['mm'])
            st['pb_guard'][par] = P.c['mm']
        blk.tensor(tbody)

        def vbody(v):
            for kt in range(ntile):
                v.wait_ge(P.sems['mm'], rec['tr_mm'][kt])
                ins = v.tensor_copy(cfg.PT[:, kt, qt4 * 128:(qt4 + 1) * 128],
                                    trslot(kt))
                P.inc(ins, 've', 1)
            assert P.c['ve'] == ev_ve[-1]
        blk.vector(vbody)

    def emit_av(h, qg, qt_globals):
        gmax = 16 if qg == 0 else 32
        av_mm = []

        def tbody(t):
            last_ve = max(qstate[q]['ev_ve'][-1] for q in qt_globals)
            t.wait_ge(P.sems['ve'], last_ve)
            slot = (2 * h + qg) % 2
            if st['pt_guard'] > 0 and False:
                pass
            prev = st.get('av_writer_%d' % slot)
            if prev:
                t.wait_ge(P.sems['epv'], prev)
            for kt in range(gmax):
                ins = t.matmul(cfg.ps_av[:, slot, :],
                               cfg.v_lhsT(h, kt), cfg.PT[:, kt, :],
                               start=(kt == 0), stop=(kt == gmax - 1))
            P.inc(ins, 'mm', 1)
            av_mm.append(P.c['mm'])
            st['pt_guard'] = P.c['mm']
            st['av_writer_%d' % slot] = P.c['epv'] + 1
        blk.tensor(tbody)

        def vbody(v):
            v.wait_ge(P.sems['mm'], av_mm[0])
            cfg.out_writer(v, h, qg, cfg.ps_av[:, (2 * h + qg) % 2, :])
        blk.vector(vbody)

    for h in range(cfg.heads):
        cfg.head_loads(h, head_av_end.get(h - 2, 0))
        for qg in range(2):
            q0 = st['qt']
            for qt4 in range(4):
                scores_softmax(h, qg, qt4)
                if qt4 > 0:
                    transpose_evac(q0 + qt4 - 1, qt4 == 1)
            transpose_evac(q0 + 3, False)
            emit_av(h, qg, [q0, q0 + 1, q0 + 2, q0 + 3])
        head_av_end[h] = P.c['mm']
        if getattr(cfg, 'post_head', None) is not None:
            cfg.post_head(h)




def build_launch_A():
    nc = bass.Bass("TRN2", target_bir_lowering=False,
                   detect_race_conditions=_RACE_DETECT)
    x0 = nc.dram_tensor("x0", [D, TPC], F32, kind="ExternalInput").ap()
    wqk = nc.dram_tensor("wqk", [16, 128, 32, 128], F32, kind="ExternalInput").ap()
    wv = nc.dram_tensor("wv", [128, 32, 1024], F32, kind="ExternalInput").ap()
    bqk = nc.dram_tensor("bqk", [128, 16], F32, kind="ExternalInput").ap()
    bv = nc.dram_tensor("bv", [1, 1024], F32, kind="ExternalInput").ap()
    q1o = nc.dram_tensor("q1o", [1024, TPC], BF16, kind="ExternalOutput").ap()
    k1o = nc.dram_tensor("k1o", [1024, TPC], BF16, kind="ExternalOutput").ap()
    v1o = nc.dram_tensor("v1o", [TPC, 1024], BF16, kind="ExternalOutput").ap()

    with contextlib.ExitStack() as stack:
        P = Prog(nc, stack)
        for s in ('gld', 'sld', 'mm', 'se', 've', 'ep', 'st'):
            P.sem(s)
        xn = P.sb("xn", [128, NCC, 1024], BF16)
        xsq = P.sb("xsq", [128, 2, 1024], BF16)
        xf32 = P.sb("xf32", [128, 2, 1024], F32)
        tmpf = P.sb("tmpf", [128, 2, 1024], F32)
        wbuf = P.sb("wbuf", [128, 2, 32, 128], BF16)
        wvb = P.sb("wvb", [128, 32, 1024], BF16)
        otile = P.sb("otile", [128, 4, 512], BF16)
        ones128_bf = P.sb("ones128_bf", [128, 1], BF16)
        ones1_bf = P.sb("ones1_bf", [1, 128], BF16)
        ones1_f32 = P.sb("ones1_f32", [1, 128], F32)
        bqk_sb = P.sb("bqk_sb", [128, 16], F32)
        bv_sb = P.sb("bv_sb", [1, 1024], BF16)
        mu_row = P.sb("mu_row", [1, 1024], F32)
        rs_row = P.sb("rs_row", [1, 1024], F32)
        rows_tmp = P.sb("rows_tmp", [1, 1024], F32)
        MUb = P.sb("MUb", [128, 1024], F32)
        RSb = P.sb("RSb", [128, 1024], F32)

        ps_st = P.ps("ps_st", [1, 4, 512], F32)
        ps_out = P.ps("ps_out", [128, 2, 512], F32)
        ps_v = P.ps("ps_v", [128, 2, 512], F32)
        ps_bc = ps_out

        with nc.Block() as blk:
            def vinit(v):
                P.inc(v.memset(ones128_bf[:, :], 1.0), 've', 1)
                P.inc(v.memset(ones1_bf[0:1, :], 1.0), 've', 1)
                P.inc(v.memset(ones1_f32[0:1, :], 1.0), 've', 1)
            blk.vector(vinit)
            P.marks['consts_ve'] = P.c['ve']

            P.sem('ini')
            P.sem('cst')

            def sinit(s):
                P.inc(s.dma_start(out=bqk_sb[:, :], in_=bqk), 'ini', 16)
            blk.sync(sinit)

            def ginit(g):
                P.inc(g.dma_start(out=bv_sb[0:1, :], in_=bv), 'cst', 16)
                P.inc(g.dma_start(out=wvb[:, :, :], in_=wv), 'cst', 16)
            blk.gpsimd(ginit)

            emit_ln(P, blk, x0, xn, xsq, xf32, tmpf, xn, ones128_bf,
                    ps_st, mu_row, rs_row, rows_tmp, MUb, RSb, ps_bc, ones1_f32)

            P.chan('sto', 4)

            def ep_scalar(sc, ob, th, ps_ap, k, mmv):
                sc.wait_ge(P.sems['mm'], mmv)
                if k == 0:
                    P.wait(sc, 'ini')
                oslot = k % 4
                sc.wait_ge(P.sems[f'sto{oslot}'],
                           sto_base[0][oslot] + 16 * (k // 4))
                ins = sc.activation(otile[:, oslot, :], ps_ap, AF.Identity,
                                    bias=bqk_sb[:, ob:ob + 1])
                P.inc(ins, 'eps', 1)

            def ep_sync(s, ob, th, ps_ap, k, mmv):
                s.wait_ge(P.sems['eps'], ep_base[0] + k + 1)
                oslot = k % 4
                dst = q1o if ob < 8 else k1o
                rb = (ob % 8) * 128
                ins = s.dma_start(out=dst[rb:rb + 128, th * 512:(th + 1) * 512],
                                  in_=otile[:, oslot, :])
                P.inc(ins, f'sto{oslot}', 16)

            P.sem('eps')
            ep_base = [P.c['eps']]
            sto_base = [[P.c[f'sto{i}'] for i in range(4)]]
            emit_proj_fm(P, blk, wqk, D, 2048, lambda cc: xn[:, cc, :], wbuf,
                         ps_out, ep_scalar=ep_scalar, ep_sync=ep_sync,
                         twaits=(('ve', P.c['ve']),))

            def tbody(t):
                P.wait(t, 'cst')
                t.wait_ge(P.sems['ve'], P.marks['consts_ve'])
                mmv = P.c['mm']
                P.sem('epv')
                ep_b = P.c['epv']
                k = 0
                for fb in range(2):
                    for th in range(8):
                        pslot = k % 2
                        if k >= 2:
                            t.wait_ge(P.sems['epv'], ep_b + (k - 1))
                        for cc in range(NCC):
                            t.matmul(ps_v[:, pslot, :],
                                     xn[:, cc, th * 128:(th + 1) * 128],
                                     wvb[:, cc, fb * 512:(fb + 1) * 512],
                                     start=(cc == 0), stop=False)
                        ins = t.matmul(ps_v[:, pslot, :], ones1_bf[0:1, 0:128],
                                       bv_sb[0:1, fb * 512:(fb + 1) * 512],
                                       start=False, stop=True)
                        P.inc(ins, 'mm', 1)
                        k += 1
            blk.tensor(tbody)

            sto_b2 = [P.c[f'sto{i}'] for i in range(4)]

            def vbody(v):
                mm_b = P.c['mm'] - 16
                k = 0
                for fb in range(2):
                    for th in range(8):
                        oslot = k % 4
                        v.wait_ge(P.sems['mm'], mm_b + k + 1)
                        v.wait_ge(P.sems[f'sto{oslot}'],
                                  sto_b2[oslot] + 16 * (k // 4))
                        ins = v.tensor_copy(otile[:, oslot, :], ps_v[:, k % 2, :])
                        P.inc(ins, 'epv', 1)
                        k += 1
            blk.vector(vbody)

            def sybody(s):
                ep_b = P.c['epv'] - 16
                k = 0
                for fb in range(2):
                    for th in range(8):
                        oslot = k % 4
                        s.wait_ge(P.sems['epv'], ep_b + k + 1)
                        ins = s.dma_start(
                            out=v1o[th * 128:(th + 1) * 128, fb * 512:(fb + 1) * 512],
                            in_=otile[:, oslot, :])
                        P.inc(ins, f'sto{oslot}', 16)
                        k += 1
                for i in range(4):
                    P.wait(s, f'sto{i}')
            blk.sync(sybody)

    return nc


def emit_ffn_block(P, blk, nc, w12, w3, b12_sb, b3_sb, x_old_hbm, x_new_hbm,
                   ones128_bf, ones1_f32, mu_row, rs_row, rows_tmp, MUb, RSb,
                   with_stats):
    with contextlib.ExitStack() as ph:
        Pph = lambda name, shape, dt: ph.enter_context(
            nc.sbuf_tensor(name, shape, dt))
        x4 = Pph("x4", [128, 32, 1024], BF16)
        wbuf = Pph("wbufb", [128, 2, 32, 128], BF16)
        xf32 = Pph("xf32b", [128, 4, 512], F32)
        t1 = Pph("t1", [128, 4, 512], F32)
        yrot = t1[:, 0:2, :]
        ps_out = ph.enter_context(nc.psum_tensor("ps_out2", [128, 4, 512], F32))

        scope_a = contextlib.ExitStack()
        xn = scope_a.enter_context(nc.sbuf_tensor("xn", [128, 32, 1024], BF16))
        xfln = xf32.ap().rearrange("p a b -> p (a b)").rearrange(
            "p (a b) -> p a b", a=2)
        tmpf = scope_a.enter_context(nc.sbuf_tensor("tmpf", [128, 2, 1024], F32))

        emit_bcast_rows(P, blk, ones1_f32, mu_row, rs_row, MUb, RSb,
                        ps_out[:, 0:2, :])
        emit_ln_apply(P, blk, x_old_hbm, xn, xfln, tmpf, MUb, RSb,
                      ps_out, ones1_f32)

        ob_order = []
        for j in range(32):
            ob_order += [j, j + 32]

        def ep_scalar(sc, ob, th, ps_ap, k, mmv):
            sc.wait_ge(P.sems['mm'], mmv)
            if ob < 32:
                if k >= 4:
                    sc.wait_ge(P.sems['epv'], ffn_epv0[0]
                               + 2 * ((k - 4) // 4) + ((k - 4) % 4) + 1)
                ins = sc.activation(t1[:, k % 4, :], ps_ap, AF.Silu,
                                    bias=b12_sb[:, ob:ob + 1])
                P.inc(ins, 'eps', 1)

        def ep_vector(v, ob, th, ps_ap, k, mmv):
            if ob < 32:
                return
            v.wait_ge(P.sems['mm'], mmv)
            v.wait_ge(P.sems['eps'], ffn_ep0[0] + (k - 2) // 4 * 2
                      + (k - 2) % 4 + 1)
            ins = v.scalar_tensor_tensor(
                x4[:, ob - 32, th * 512:(th + 1) * 512], ps_ap,
                b12_sb[:, ob:ob + 1], t1[:, (k - 2) % 4, :],
                ALU.add, ALU.mult)
            P.inc(ins, 'epv', 1)

        P.sem('eps')
        P.sem('epv')
        ffn_ep0 = [P.c['eps']]
        ffn_epv0 = [P.c['epv']]
        emit_proj_fm(P, blk, w12, D, 8192,
                     lambda cc: xn[:, cc, :], wbuf, ps_out,
                     ep_scalar=ep_scalar, ep_vector=ep_vector,
                     ob_order=ob_order, twaits=(('ve', P.c['ve']),),
                     ep_sem_of=lambda ob, k: 'eps' if ob < 32 else 'epv')
        scope_a.close()

        if with_stats:
            x2b = Pph("x2b", [128, 32, 1024], BF16)
            xsq = Pph("xsqb", [128, 2, 1024], BF16)
        else:
            x2b = None
            xsq = None
        emit_proj_residual(P, blk, w3, D, x_old_hbm, x_new_hbm, x2b, xf32,
                           yrot, wbuf, ps_out[:, 0:2, :],
                           bias_sb=b3_sb,
                           rhs_of_cc=lambda cc: x4[:, cc, :],
                           twaits=(('epv', P.c['epv']),
                                   ('eps', P.c['eps'])))
        if with_stats:
            ps_st = ph.enter_context(nc.psum_tensor("ps_st2", [1, 4, 512], F32))
            emit_stats_from_sbuf(P, blk, x2b, xsq, ps_st, ones128_bf,
                                 pre_wait=('ve', P.c['ve']))
            emit_ln_stats_finalize_ps(P, blk, ps_st, mu_row, rs_row, rows_tmp)




def build_launch_B(upto=3):
    nc = bass.Bass("TRN2", target_bir_lowering=False,
                   detect_race_conditions=_RACE_DETECT)
    x0 = nc.dram_tensor("x0", [D, TPC], F32, kind="ExternalInput").ap()
    q1i = nc.dram_tensor("q1i", [8, 128, TPC], BF16, kind="ExternalInput").ap()
    k1g = nc.dram_tensor("k1g", [8, 128, S], BF16, kind="ExternalInput").ap()
    v1g = nc.dram_tensor("v1g", [8, 128, 32, 128], BF16, kind="ExternalInput").ap()
    wo_a = nc.dram_tensor("wo_a", [32, 128, 8, 128], F32, kind="ExternalInput").ap()
    bo = nc.dram_tensor("bo", [128, 32], F32, kind="ExternalInput").ap()
    w12 = nc.dram_tensor("w12", [64, 128, 32, 128], F32, kind="ExternalInput").ap()
    b12 = nc.dram_tensor("b12", [128, 64], F32, kind="ExternalInput").ap()
    w3 = nc.dram_tensor("w3", [32, 128, 32, 128], F32, kind="ExternalInput").ap()
    b3 = nc.dram_tensor("b3", [128, 32], F32, kind="ExternalInput").ap()
    wq3 = nc.dram_tensor("wq3", [32, 128, 32, 128], F32, kind="ExternalInput").ap()
    wk3 = nc.dram_tensor("wk3", [32, 128, 32, 128], F32, kind="ExternalInput").ap()
    wv3 = nc.dram_tensor("wv3", [128, 32, D], F32, kind="ExternalInput").ap()
    ropeC = nc.dram_tensor("ropeC", [128, TPC], F32, kind="ExternalInput").ap()
    ropeS = nc.dram_tensor("ropeS", [128, TPC], F32, kind="ExternalInput").ap()
    ident_i = nc.dram_tensor("ident_i", [128, 128], BF16, kind="ExternalInput").ap()
    x1s = nc.dram_tensor("x1s", [D, TPC], F32, kind="Internal").ap()
    x2o = nc.dram_tensor("x2o", [D, TPC], F32, kind="ExternalOutput").ap()
    q3o = nc.dram_tensor("q3o", [D, TPC], BF16, kind="ExternalOutput").ap()
    k3o = nc.dram_tensor("k3o", [D, TPC], BF16, kind="ExternalOutput").ap()
    v3o = nc.dram_tensor("v3o", [TPC, D], BF16, kind="ExternalOutput").ap()

    with contextlib.ExitStack() as stack:
        P = Prog(nc, stack)
        for sm in ('gld', 'sld', 'mm', 'se', 've', 'ep', 'st', 'sp', 'xp', 'pn'):
            P.sem(sm)
        otile = P.sb("otile", [128, 4, 512], BF16)
        ident = P.sb("ident", [128, 128], BF16)
        ones128_bf = P.sb("ones128_bf", [128, 1], BF16)
        ones1_f32 = P.sb("ones1_f32", [1, 128], F32)
        bo_sb = P.sb("bo_sb", [128, 32], F32)
        b12_sb = P.sb("b12_sb", [128, 64], F32)
        b3_sb = P.sb("b3_sb", [128, 32], F32)
        mu_row = P.sb("mu_row", [1, 1024], F32)
        rs_row = P.sb("rs_row", [1, 1024], F32)
        rows_tmp = P.sb("rows_tmp", [1, 1024], F32)
        MUb = P.sb("MUb", [128, 1024], F32)
        RSb = P.sb("RSb", [128, 1024], F32)
        def c_init(blkk):
            def vinit(v):
                P.inc(v.memset(ones128_bf[:, :], 1.0), 've', 1)
                P.inc(v.memset(ones1_f32[0:1, :], 1.0), 've', 1)
            blkk.vector(vinit)
            P.marks['consts_ve'] = P.c['ve']

            P.sem('ini')

            def syinit(sy):
                for ap_, src in ((ident, ident_i), (bo_sb, bo),
                                 (b12_sb, b12), (b3_sb, b3)):
                    P.inc(sy.dma_start(out=ap_[:, :], in_=src), 'ini', 16)
            blkk.sync(syinit)

        with nc.Block() as blk:
            c_init(blk)

            with contextlib.ExitStack() as ph:
                Pph = lambda name, shape, dt: ph.enter_context(
                    nc.sbuf_tensor(name, shape, dt))
                ao = Pph("ao", [128, 8, 1024], BF16)
                attn_scope = contextlib.ExitStack()
                Pat = lambda name, shape, dt: attn_scope.enter_context(
                    nc.sbuf_tensor(name, shape, dt))
                q1sb = Pat("q1sb", [128, 8, 1024], BF16)
                Ksb = Pat("Ksb", [128, 8, S], BF16)
                Vbuf = Pat("Vbuf", [128, 2, 32, 128], BF16)
                S_sb = Pat("S_sb", [128, 2048], F32)
                Pb = Pat("Pb", [128, 2, 4096], BF16)
                PT = Pat("PT", [128, 32, 512], BF16)
                mpart = Pat("mpart", [128, 2, 8], F32)
                Dpart = Pat("Dpart", [128, 2, 8], F32)
                mrow = Pat("mrow", [128, 2, 1], F32)
                negm = Pat("negm", [128, 2, 1], F32)
                Dsum = Pat("Dsum", [128, 2, 1], F32)
                rD = Pat("rD", [128, 2, 1], F32)
                ps_sc = attn_scope.enter_context(nc.psum_tensor("ps_sc", [128, 4, 512], F32))
                ps_tr = attn_scope.enter_context(nc.psum_tensor("ps_tr", [128, 2, 1024], BF16))
                ps_av = attn_scope.enter_context(nc.psum_tensor("ps_av", [128, 2, 512], F32))

                P.sem('ka')
                sldK = [0]

                def head_loads(h, guard):
                    def sybody(sy):
                        for j in range(4):
                            ins = sy.dma_start(
                                out=Ksb[:, j * 2:(j + 1) * 2, :],
                                in_=k1g[j * 2:(j + 1) * 2].rearrange(
                                    "dc p k -> p dc k"))
                            P.inc(ins, 'ka', 16)
                        ins = sy.dma_start(out=q1sb[:, :, :],
                                           in_=q1i.rearrange("dc p t -> p dc t"))
                        P.inc(ins, 'ka', 16)
                        sldK[0] = P.c['ka']
                    blk.sync(sybody)

                def k_wait(t, h):
                    t.wait_ge(P.sems['ka'], sldK[0])
                    P.wait(t, 'ini')

                P.chan('vb', 2)
                prev_av = [None]

                def v_load(sy, qg, av_mm):
                    for dc in range(8):
                        if dc >= 2:
                            sy.wait_ge(P.sems['mm'], av_mm[dc - 2])
                        elif prev_av[0] is not None:
                            sy.wait_ge(P.sems['mm'], prev_av[0][6 + dc])
                        ins = sy.dma_start(out=Vbuf[:, dc % 2, :, :],
                                           in_=v1g[dc])
                        P.inc(ins, f'vb{dc % 2}', 16)
                    prev_av[0] = av_mm

                vbase = [0, 0]

                def v_wait(t, h, qg, dc, v_sld_base):
                    if dc == 0:
                        vbase[0] = P.c['vb0']
                        vbase[1] = P.c['vb1']
                    t.wait_ge(P.sems[f'vb{dc % 2}'],
                              vbase[dc % 2] + 16 * (dc // 2 + 1))

                def out_writer(v, h, qg, dc, ps_ap):
                    ins = v.tensor_copy(ao[:, dc, qg * 512:(qg + 1) * 512],
                                        ps_ap)
                    P.inc(ins, 'epv', 1)

                cfg = AttnCfg(
                    heads=1, HC=8, causal=False, scale=SC_A,
                    block_ids=[0, 0], n_qg=2,
                    q_lhsT=lambda h, qg, qt4, dc: q1sb[:, dc, (qg * 4 + qt4) * 128:(qg * 4 + qt4 + 1) * 128],
                    k_rhs=lambda h, kc, dc: Ksb[:, dc, kc * 512:(kc + 1) * 512],
                    v_lhsT=lambda h, kt, dc: Vbuf[:, dc % 2, kt, :],
                    head_loads=head_loads, k_wait=k_wait, v_load=v_load,
                    v_wait=v_wait, out_writer=out_writer,
                    ps_sc=ps_sc, ps_tr=ps_tr, ps_av=ps_av, S_sb=S_sb, Pb=Pb,
                    PT=PT, mpart=mpart, Dpart=Dpart, mrow=mrow, negm=negm,
                    Dsum=Dsum, rD=rD, ident=ident, mask_sb=None,
                )
                emit_attention(P, blk, cfg)
                if upto < 1:
                    def _dump(sy):
                        P.wait(sy, 'epv')
                        for cc in range(8):
                            P.inc(sy.dma_start(
                                out=q3o[cc * 128:(cc + 1) * 128, :],
                                in_=ao[:, cc, :]), 'st', 16)
                        P.wait(sy, 'st')
                    blk.sync(_dump)
                    attn_scope.close()
                    return nc
                attn_scope.close()
                emit_barrier(P, blk)

                x1b = Pph("x1b", [128, 32, 1024], BF16)
                xf32 = Pph("xf32", [128, 4, 512], F32)
                yrot = Pph("yrot", [128, 2, 512], F32)
                wbuf8 = Pph("wbuf8", [128, 2, 8, 128], BF16)
                xsq = Pph("xsq", [128, 2, 1024], BF16)
                ps_out = ph.enter_context(nc.psum_tensor("ps_out", [128, 2, 512], F32))
                ps_st = ph.enter_context(nc.psum_tensor("ps_st", [1, 4, 512], F32))

                x1dst = x1s if upto >= 2 else x2o
                emit_proj_residual(P, blk, wo_a, 1024, x0, x1dst, x1b, xf32,
                                   yrot, wbuf8, ps_out, bias_sb=bo_sb,
                                   rhs_of_cc=lambda cc: ao[:, cc, :],
                                   twaits=(('epv', P.c['epv']),))
                emit_stats_from_sbuf(P, blk, x1b, xsq, ps_st, ones128_bf,
                                     pre_wait=('ve', P.c['ve']))
                emit_ln_stats_finalize_ps(P, blk, ps_st, mu_row, rs_row,
                                          rows_tmp)

            emit_barrier(P, blk)
            if upto < 2:
                def _fin(sy):
                    for i in range(4):
                        if f'xr{i}' in P.sems:
                            P.wait(sy, f'xr{i}')
                blk.sync(_fin)
                return nc
            emit_ffn_block(P, blk, nc, w12, w3, b12_sb, b3_sb, x1s, x2o,
                           ones128_bf, ones1_f32, mu_row, rs_row, rows_tmp,
                           MUb, RSb, with_stats=True)

            emit_barrier(P, blk)
            if upto < 3:
                def _fin(sy):
                    P.wait(sy, 'st')
                blk.sync(_fin)
                return nc
            with contextlib.ExitStack() as ph:
                Pph = lambda name, shape, dt: ph.enter_context(
                    nc.sbuf_tensor(name, shape, dt))
                xn = Pph("xn3", [128, 32, 1024], BF16)
                xf32 = Pph("xf32c", [128, 2, 1024], F32)
                tmpf = Pph("tmpfc", [128, 2, 1024], F32)
                wbuf = Pph("wbufc", [128, 2, 32, 128], BF16)
                wvb = Pph("wvb", [128, 2, 32, 512], BF16)
                C2 = Pph("C2", [128, 1024], F32)
                S2 = Pph("S2", [128, 1024], F32)
                rta = Pph("rta", [128, 2, 512], F32)
                rtb = Pph("rtb", [128, 2, 512], F32)
                ps_out = ph.enter_context(nc.psum_tensor("ps_out3", [128, 4, 512], F32))

                def syinit2(sy):
                    P.inc(sy.dma_start(out=C2[:, :], in_=ropeC), 'ini', 16)
                    P.inc(sy.dma_start(out=S2[:, :], in_=ropeS), 'ini', 16)
                blk.sync(syinit2)
                rope_ready = P.c['ini']

                emit_bcast_rows(P, blk, ones1_f32, mu_row, rs_row, MUb, RSb,
                                ps_out[:, 0:2, :])
                emit_ln_apply(P, blk, x2o, xn, xf32, tmpf, MUb, RSb, ps_out,
                              ones1_f32)

                P.chan('sto', 4)

                def mk_rope_ep(dst):
                    state = {'sto0': [P.c[f'sto{i}'] for i in range(4)]}

                    def ep_vector(v, ob, th, ps_ap, k, mmv):
                        v.wait_ge(P.sems['mm'], mmv)
                        if k == 0:
                            v.wait_ge(P.sems['ini'], rope_ready)
                        slot = k % 2
                        oslot = k % 4
                        v.wait_ge(P.sems[f'sto{oslot}'],
                                  state['sto0'][oslot] + 16 * (k // 4))
                        tsl = slice(th * 512, (th + 1) * 512)
                        ins = v.tensor_mul(rta[:, slot, :], ps_ap, C2[:, tsl])
                        P.edge(ins, 'dv')
                        ins = v.tensor_copy(rtb[0:64, slot, :], ps_ap[64:128, :])
                        P.edge(ins, 'dv')
                        ins = v.tensor_copy(rtb[64:128, slot, :], ps_ap[0:64, :])
                        P.edge(ins, 'dv')
                        P.join(v, 'dv')
                        ins = v.tensor_mul(rtb[:, slot, :], rtb[:, slot, :],
                                           S2[:, tsl])
                        P.edge(ins, 'dv')
                        P.join(v, 'dv')
                        ins = v.tensor_add(otile[:, oslot, :], rta[:, slot, :],
                                           rtb[:, slot, :])
                        P.inc(ins, 'epv', 1)

                    def ep_sync(sy, ob, th, ps_ap, k, mmv):
                        sy.wait_ge(P.sems['epv'], rope_ep0[0] + k + 1)
                        ins = sy.dma_start(
                            out=dst[ob * 128:(ob + 1) * 128,
                                    th * 512:(th + 1) * 512],
                            in_=otile[:, k % 4, :])
                        P.inc(ins, f'sto{k % 4}', 16)
                    return ep_vector, ep_sync

                P.sem('epv')
                for wsrc, dst in ((wq3, q3o), (wk3, k3o)):
                    epv, epsy = mk_rope_ep(dst)
                    rope_ep0 = [P.c['epv']]
                    emit_proj_fm(P, blk, wsrc, D, D, lambda cc: xn[:, cc, :],
                                 wbuf, ps_out[:, 0:2, :], ep_vector=epv,
                                 ep_sync=epsy, twaits=(('ve', P.c['ve']),),
                                 ep_sem_of=lambda ob, k: 'epv')

                def tbody(t):
                    P.wait(t, 've')
                    ep_b = P.c['epv']
                    k = 0
                    for fb in range(8):
                        t.wait_ge(P.sems[f'wld{fb % 2}'],
                                  wv30[fb % 2] + 16 * (fb // 2 + 1))
                        for th in range(8):
                            pslot = 2 + (k % 2)
                            if k >= 2:
                                t.wait_ge(P.sems['epv'], ep_b + (k - 1))
                            for cc in range(NCC):
                                ins = t.matmul(ps_out[:, pslot, :],
                                               xn[:, cc, th * 128:(th + 1) * 128],
                                               wvb[:, fb % 2, cc, :],
                                               start=(cc == 0),
                                               stop=(cc == NCC - 1))
                            P.inc(ins, 'mm', 1)
                            v3mm.append(P.c['mm'])
                            k += 1

                def gbody(g):
                    for fb in range(8):
                        if fb >= 2:
                            g.wait_ge(P.sems['mm'], v3guard[0] + (fb - 1) * 8)
                        else:
                            g.wait_ge(P.sems['mm'], v3guard[0])
                        ins = g.dma_start(out=wvb[:, fb % 2, :, :],
                                          in_=wv3[:, :, fb * 512:(fb + 1) * 512])
                        P.inc(ins, f'wld{fb % 2}', 16)

                v3mm = []
                v3guard = [P.c['mm']]
                wv30 = [P.c['wld0'], P.c['wld1']]
                blk.gpsimd(gbody)
                blk.tensor(tbody)

                def vbody(v):
                    sto0 = [P.c[f'sto{i}'] for i in range(4)]
                    k = 0
                    for fb in range(8):
                        for th in range(8):
                            v.wait_ge(P.sems['mm'], v3mm[k])
                            oslot = k % 4
                            v.wait_ge(P.sems[f'sto{oslot}'],
                                      sto0[oslot] + 16 * (k // 4))
                            ins = v.tensor_copy(otile[:, oslot, :],
                                                ps_out[:, 2 + (k % 2), :])
                            P.inc(ins, 'epv', 1)
                            k += 1
                blk.vector(vbody)

                def sybody(sy):
                    ep_b = P.c['epv'] - 64
                    k = 0
                    for fb in range(8):
                        for th in range(8):
                            sy.wait_ge(P.sems['epv'], ep_b + k + 1)
                            ins = sy.dma_start(
                                out=v3o[th * 128:(th + 1) * 128,
                                        fb * 512:(fb + 1) * 512],
                                in_=otile[:, k % 4, :])
                            P.inc(ins, f'sto{k % 4}', 16)
                            k += 1
                    for i in range(4):
                        P.wait(sy, f'sto{i}')
                    P.wait(sy, 'st')
                blk.sync(sybody)

    return nc


def prep_B(inputs, resA):
    x = np.asarray(inputs["L0_attn_a_x"], np.float32).reshape(B * S, D)
    Wo = np.asarray(inputs["L0_attn_a_Wo"], np.float32)
    bo_ = np.asarray(inputs["L0_attn_a_bo"], np.float32)
    w12_ = np.asarray(inputs["L0_ffn_w12"], np.float32)
    b12_ = np.asarray(inputs["L0_ffn_b12"], np.float32)
    w3_ = np.asarray(inputs["L0_ffn_w3"], np.float32)
    b3_ = np.asarray(inputs["L0_ffn_b3"], np.float32)
    wq = np.asarray(inputs["L0_attn_b_wq"], np.float32)
    wk = np.asarray(inputs["L0_attn_b_wk"], np.float32)
    wv = np.asarray(inputs["L0_attn_b_wv"], np.float32)
    perm = head_perm()
    wo_p = pack_lhsT(Wo, 1024, D)
    w12_p = pack_lhsT(np.ascontiguousarray(w12_.T), D, 8192)
    w3_p = pack_lhsT(np.ascontiguousarray(w3_.T), D, D)
    wq_p = pack_lhsT(np.ascontiguousarray(wq[:, perm]), D, D)
    wk_p = pack_lhsT(np.ascontiguousarray(wk[:, perm]), D, D)
    wv_p = pack_rhs(wv, D, D)
    bo_c = np.ascontiguousarray(bo_.reshape(32, 128).T)
    b12_c = np.ascontiguousarray(b12_.reshape(64, 128).T)
    b3_c = np.ascontiguousarray(b3_.reshape(32, 128).T)
    ident = np.eye(128, dtype=np.float32).astype(BF)

    K1 = {}
    V1 = {}
    for b in range(B):
        Kf = np.zeros((1024, S), BF)
        Vt = np.zeros((S, 1024), BF)
        for r in range(4):
            core = b * 4 + r
            k1 = resA.results[core]["k1o"]
            v1 = resA.results[core]["v1o"]
            for j, blkid in enumerate(blocks_of(r)):
                Kf[:, blkid * BLK:(blkid + 1) * BLK] = k1[:, j * BLK:(j + 1) * BLK]
                Vt[blkid * BLK:(blkid + 1) * BLK, :] = v1[j * BLK:(j + 1) * BLK, :]
        K1[b] = np.ascontiguousarray(Kf.reshape(8, 128, S))
        V1[b] = np.ascontiguousarray(
            Vt.reshape(32, 128, 8, 128).transpose(2, 1, 0, 3))
    in_maps = []
    for core in range(8):
        b, r = core // 4, core % 4
        xs = np.ascontiguousarray(_slice_tokens(x, b, r).T)
        lo, hi = blocks_of(r)
        pos = np.concatenate([np.arange(lo * BLK, (lo + 1) * BLK),
                              np.arange(hi * BLK, (hi + 1) * BLK)])
        C2, S2 = rope_tables(pos)
        in_maps.append({
            "x0": xs,
            "q1i": np.ascontiguousarray(
                resA.results[core]["q1o"].reshape(8, 128, TPC)),
            "k1g": K1[b], "v1g": V1[b],
            "wo_a": wo_p, "bo": bo_c, "w12": w12_p, "b12": b12_c,
            "w3": w3_p, "b3": b3_c, "wq3": wq_p, "wk3": wk_p, "wv3": wv_p,
            "ropeC": C2, "ropeS": S2, "ident_i": ident,
        })
    return in_maps



def build_launch_C():
    nc = bass.Bass("TRN2", target_bir_lowering=False,
                   detect_race_conditions=_RACE_DETECT)
    x2 = nc.dram_tensor("x2", [D, TPC], F32, kind="ExternalInput").ap()
    q3i = nc.dram_tensor("q3i", [32, 128, TPC], BF16, kind="ExternalInput").ap()
    k3g = nc.dram_tensor("k3g", [32, 128, S], BF16, kind="ExternalInput").ap()
    v3g = nc.dram_tensor("v3g", [32, 128, 32, 128], BF16, kind="ExternalInput").ap()
    wo_b = nc.dram_tensor("wo_b", [32, 128, 32, 128], F32, kind="ExternalInput").ap()
    w12 = nc.dram_tensor("w12", [64, 128, 32, 128], F32, kind="ExternalInput").ap()
    b12 = nc.dram_tensor("b12", [128, 64], F32, kind="ExternalInput").ap()
    w3 = nc.dram_tensor("w3", [32, 128, 32, 128], F32, kind="ExternalInput").ap()
    b3 = nc.dram_tensor("b3", [128, 32], F32, kind="ExternalInput").ap()
    masks = nc.dram_tensor("masks", [128, 4, 2, 2048], BF16, kind="ExternalInput").ap()
    ident_i = nc.dram_tensor("ident_i", [128, 128], BF16, kind="ExternalInput").ap()
    x3s = nc.dram_tensor("x3s", [D, TPC], F32, kind="Internal").ap()
    ao_s = nc.dram_tensor("ao_s", [D, TPC], BF16, kind="Internal").ap()
    x4o = nc.dram_tensor("x4o", [D, TPC], F32, kind="ExternalOutput").ap()

    with contextlib.ExitStack() as stack:
        P = Prog(nc, stack)
        for sm in ('gld', 'mm', 'se', 've', 'st', 'sp', 'xp', 'pn'):
            P.sem(sm)
        ident = P.sb("ident", [128, 128], BF16)
        ones128_bf = P.sb("ones128_bf", [128, 1], BF16)
        ones1_f32 = P.sb("ones1_f32", [1, 128], F32)
        b12_sb = P.sb("b12_sb", [128, 64], F32)
        b3_sb = P.sb("b3_sb", [128, 32], F32)
        mu_row = P.sb("mu_row", [1, 1024], F32)
        rs_row = P.sb("rs_row", [1, 1024], F32)
        rows_tmp = P.sb("rows_tmp", [1, 1024], F32)
        MUb = P.sb("MUb", [128, 1024], F32)
        RSb = P.sb("RSb", [128, 1024], F32)


        with nc.Block() as blk:
            P.sem('ini')

            def vinit(v):
                P.inc(v.memset(ones128_bf[:, :], 1.0), 've', 1)
                P.inc(v.memset(ones1_f32[0:1, :], 1.0), 've', 1)
            blk.vector(vinit)
            P.marks['consts_ve'] = P.c['ve']

            def syinit(sy):
                for ap_, src in ((ident, ident_i), (b12_sb, b12),
                                 (b3_sb, b3)):
                    P.inc(sy.dma_start(out=ap_[:, :], in_=src), 'ini', 16)
            blk.sync(syinit)

            with contextlib.ExitStack() as ph:
                Pph = lambda name, shape, dt: ph.enter_context(
                    nc.sbuf_tensor(name, shape, dt))
                attn_scope = contextlib.ExitStack()
                Pat = lambda name, shape, dt: attn_scope.enter_context(
                    nc.sbuf_tensor(name, shape, dt))
                mask_sb = Pat("mask_sb", [128, 4, 2, 2048], BF16)
                otile = Pat("otile", [128, 4, 512], BF16)
                qsb = Pat("qsb", [128, 2, 1024], BF16)
                Ksb = Pat("Ksb", [128, 2, S], BF16)
                Vbuf = Pat("Vbuf", [128, 2, 32, 128], BF16)
                S_sb = Pat("S_sb", [128, 2048], F32)
                Pb = Pat("Pb", [128, 2, 4096], BF16)
                PT = Pat("PT", [128, 32, 512], BF16)
                mpart = Pat("mpart", [128, 2, 8], F32)
                Dpart = Pat("Dpart", [128, 2, 8], F32)
                mrow = Pat("mrow", [128, 2, 1], F32)
                negm = Pat("negm", [128, 2, 1], F32)
                Dsum = Pat("Dsum", [128, 2, 1], F32)
                rD = Pat("rD", [128, 2, 1], F32)
                ps_sc = attn_scope.enter_context(
                    nc.psum_tensor("ps_sc", [128, 4, 512], F32))
                ps_tr = attn_scope.enter_context(
                    nc.psum_tensor("ps_tr", [128, 2, 1024], BF16))
                ps_av = attn_scope.enter_context(
                    nc.psum_tensor("ps_av", [128, 2, 512], F32))

                def mload(sy):
                    P.inc(sy.dma_start(out=mask_sb[:, :, :, :], in_=masks),
                          'ini', 16)
                blk.sync(mload)

                P.chan('kv', 2)
                kv0 = {0: P.c['kv0'], 1: P.c['kv1']}

                def head_loads(h, guard):
                    def sybody(sy):
                        if h >= 2 and guard:
                            sy.wait_ge(P.sems['mm'], guard)
                        slot = h % 2
                        P.inc(sy.dma_start(out=Ksb[:, slot, :], in_=k3g[h]),
                              f'kv{slot}', 16)
                        P.inc(sy.dma_start(out=Vbuf[:, slot, :, :], in_=v3g[h]),
                              f'kv{slot}', 16)
                        P.inc(sy.dma_start(out=qsb[:, slot, :], in_=q3i[h]),
                              f'kv{slot}', 16)
                    blk.sync(sybody)

                def k_wait(t, h):
                    slot = h % 2
                    t.wait_ge(P.sems[f'kv{slot}'],
                              kv0[slot] + 48 * (h // 2 + 1))
                    if h == 0:
                        P.wait(t, 'ini')
                        t.wait_ge(P.sems['ve'], P.marks['consts_ve'])

                P.chan('sto', 4)
                ow_sto0 = [P.c[f'sto{i}'] for i in range(4)]
                ow_epv0 = [0]

                def out_writer(v, h, qg, ps_ap):
                    k = 2 * h + qg
                    oslot = k % 4
                    v.wait_ge(P.sems[f'sto{oslot}'],
                              ow_sto0[oslot] + 16 * (k // 4))
                    ins = v.tensor_copy(otile[:, oslot, :], ps_ap)
                    P.inc(ins, 'epv', 1)

                P.sem('epv')
                cfg = AttnCfg(
                    heads=32, scale=SC_B,
                    q_lhsT=lambda h, qg, qt4: qsb[:, h % 2, (qg * 4 + qt4) * 128:(qg * 4 + qt4 + 1) * 128],
                    k_rhs=lambda h, kc: Ksb[:, h % 2, kc * 512:(kc + 1) * 512],
                    v_lhsT=lambda h, kt: Vbuf[:, h % 2, kt, :],
                    head_loads=head_loads, k_wait=k_wait,
                    out_writer=out_writer,
                    ps_sc=ps_sc, ps_tr=ps_tr, ps_av=ps_av, S_sb=S_sb, Pb=Pb,
                    PT=PT, mpart=mpart, Dpart=Dpart, mrow=mrow, negm=negm,
                    Dsum=Dsum, rD=rD, ident=ident, mask_sb=mask_sb,
                )
                def post_head(h):
                    def body(sy):
                        for qg in range(2):
                            k = 2 * h + qg
                            sy.wait_ge(P.sems['epv'], ow_epv0[0] + k + 1)
                            ins = sy.dma_start(
                                out=ao_s[h * 128:(h + 1) * 128,
                                         qg * 512:(qg + 1) * 512],
                                in_=otile[:, k % 4, :])
                            P.inc(ins, f'sto{k % 4}', 16)
                    blk.sync(body)
                cfg.post_head = post_head

                ow_epv0[0] = P.c['epv']
                emit_attention_uniform(P, blk, cfg)
                attn_scope.close()
                emit_barrier(P, blk)

                ao = Pph("ao", [128, 32, 1024], BF16)
                x3b = Pph("x3b", [128, 32, 1024], BF16)
                xf32 = Pph("xf32", [128, 4, 512], F32)
                yrot = Pph("yrot", [128, 2, 512], F32)
                wbuf = Pph("wbuf", [128, 2, 32, 128], BF16)
                xsq = Pph("xsq", [128, 2, 1024], BF16)
                ps_out = ph.enter_context(nc.psum_tensor("ps_out", [128, 2, 512], F32))
                ps_st = ph.enter_context(nc.psum_tensor("ps_st", [1, 4, 512], F32))

                def ao_load(sy):
                    for i in range(4):
                        P.wait(sy, f'sto{i}')
                    P.sem('aol')
                    for j in range(4):
                        ins = sy.dma_start(
                            out=ao[:, j * 8:(j + 1) * 8, :],
                            in_=ao_s.rearrange("(cc p) t -> p cc t", p=128)[
                                :, j * 8:(j + 1) * 8, :])
                        P.inc(ins, 'aol', 16)
                blk.sync(ao_load)

                emit_proj_residual(P, blk, wo_b, D, x2, x3s, x3b, xf32,
                                   yrot, wbuf, ps_out, bias_sb=None,
                                   rhs_of_cc=lambda cc: ao[:, cc, :],
                                   twaits=(('aol', P.c['aol']),))
                emit_stats_from_sbuf(P, blk, x3b, xsq, ps_st, ones128_bf,
                                     pre_wait=('ve', P.c['ve']))
                emit_ln_stats_finalize_ps(P, blk, ps_st, mu_row, rs_row,
                                          rows_tmp)

            emit_barrier(P, blk)
            emit_ffn_block(P, blk, nc, w12, w3, b12_sb, b3_sb, x3s, x4o,
                           ones128_bf, ones1_f32, mu_row, rs_row, rows_tmp,
                           MUb, RSb, with_stats=False)

            def _fin(sy):
                for i in range(4):
                    if f'xr{i}' in P.sems:
                        P.wait(sy, f'xr{i}')
            blk.sync(_fin)

    return nc


def prep_C(inputs, resB):
    wo = np.asarray(inputs["L0_attn_b_wo"], np.float32)
    w12_ = np.asarray(inputs["L0_ffn_w12"], np.float32)
    b12_ = np.asarray(inputs["L0_ffn_b12"], np.float32)
    w3_ = np.asarray(inputs["L0_ffn_w3"], np.float32)
    b3_ = np.asarray(inputs["L0_ffn_b3"], np.float32)
    wo_p = pack_lhsT(wo, D, D)
    w12_p = pack_lhsT(np.ascontiguousarray(w12_.T), D, 8192)
    w3_p = pack_lhsT(np.ascontiguousarray(w3_.T), D, D)
    b12_c = np.ascontiguousarray(b12_.reshape(64, 128).T)
    b3_c = np.ascontiguousarray(b3_.reshape(32, 128).T)
    ident = np.eye(128, dtype=np.float32).astype(BF)
    NEG = np.float32(-1e30)
    tri = np.zeros((4, 128, 512), np.float32)
    for qt4 in range(4):
        for row in range(128):
            tri[qt4, row, qt4 * 128 + row + 1:] = NEG
    masks_by_rank = {}
    for r in range(4):
        m = np.zeros((128, 4, 2, 2048), np.float32)
        for qt4 in range(4):
            for cs in range(4):
                if cs == r:
                    m[:, qt4, 0, cs * 512:(cs + 1) * 512] = tri[qt4]
                elif cs > r:
                    m[:, qt4, 0, cs * 512:(cs + 1) * 512] = NEG
                kb = 4 + cs
                if kb == 7 - r:
                    m[:, qt4, 1, cs * 512:(cs + 1) * 512] = tri[qt4]
                elif kb > 7 - r:
                    m[:, qt4, 1, cs * 512:(cs + 1) * 512] = NEG
        masks_by_rank[r] = m.astype(BF)
    K3 = {}
    V3 = {}
    for b in range(B):
        Kf = np.zeros((D, S), BF)
        Vt = np.zeros((S, D), BF)
        for r in range(4):
            core = b * 4 + r
            k3 = resB.results[core]["k3o"]
            v3 = resB.results[core]["v3o"]
            for j, blkid in enumerate(blocks_of(r)):
                Kf[:, blkid * BLK:(blkid + 1) * BLK] = k3[:, j * BLK:(j + 1) * BLK]
                Vt[blkid * BLK:(blkid + 1) * BLK, :] = v3[j * BLK:(j + 1) * BLK, :]
        K3[b] = np.ascontiguousarray(Kf.reshape(32, 128, S))
        V3[b] = np.ascontiguousarray(
            Vt.reshape(32, 128, 32, 128).transpose(2, 1, 0, 3))
    in_maps = []
    for core in range(8):
        b, r = core // 4, core % 4
        in_maps.append({
            "x2": resB.results[core]["x2o"],
            "q3i": np.ascontiguousarray(
                resB.results[core]["q3o"].reshape(32, 128, TPC)),
            "k3g": K3[b], "v3g": V3[b],
            "wo_b": wo_p, "w12": w12_p, "b12": b12_c,
            "w3": w3_p, "b3": b3_c, "masks": masks_by_rank[r],
            "ident_i": ident,
        })
    return in_maps



_LAUNCH_COUNT = 0


def _run(nc, in_maps):
    global _LAUNCH_COUNT
    _LAUNCH_COUNT += 1
    return run_bass_kernel_spmd(nc, in_maps, core_ids=list(range(8)))


def _slice_tokens(arr_bt, b, r):
    lo, hi = blocks_of(r)
    return np.concatenate([arr_bt[b * S + lo * BLK: b * S + lo * BLK + BLK],
                           arr_bt[b * S + hi * BLK: b * S + hi * BLK + BLK]], 0)


def prep_A(inputs):
    x = np.asarray(inputs["L0_attn_a_x"], np.float32).reshape(B * S, D)
    Wqkv = np.asarray(inputs["L0_attn_a_Wqkv"], np.float32)
    bqkv = np.asarray(inputs["L0_attn_a_bqkv"], np.float32)
    wqk_p = pack_lhsT(Wqkv[:, :2048], D, 2048)
    wv_p = pack_rhs(Wqkv[:, 2048:], D, 1024)
    bqk = np.ascontiguousarray(bqkv[:2048].reshape(16, 128).T)
    bv = np.ascontiguousarray(bqkv[2048:].reshape(1, 1024))
    in_maps = []
    for core in range(8):
        b, r = core // 4, core % 4
        xs = np.ascontiguousarray(_slice_tokens(x, b, r).T)
        in_maps.append({"x0": xs, "wqk": wqk_p, "wv": wv_p,
                        "bqk": bqk, "bv": bv})
    return in_maps


def kernel(**inputs):
    global _LAUNCH_COUNT
    _LAUNCH_COUNT = 0
    resA = _run(build_launch_A(), prep_A(inputs))
    resB = _run(build_launch_B(), prep_B(inputs, resA))
    resC = _run(build_launch_C(), prep_C(inputs, resB))
    out = np.empty((B, S, D), np.float32)
    for core in range(8):
        b, r = core // 4, core % 4
        x4 = resC.results[core]["x4o"]
        for j, blkid in enumerate(blocks_of(r)):
            out[b, blkid * BLK:(blkid + 1) * BLK, :] = \
                x4[:, j * BLK:(j + 1) * BLK].T
    return out
